# version 09
# baseline (speedup 1.0000x reference)
import numpy as np
import ml_dtypes
import concourse.bass as bass
import concourse.mybir as mybir
from concourse.bass_utils import run_bass_kernel_spmd
from contextlib import ExitStack

F32 = mybir.dt.float32
BF16 = mybir.dt.bfloat16
AF = mybir.ActivationFunctionType
ALU = mybir.AluOpType
AX = mybir.AxisListType

D = 1024
H = 8
I = 2816
C = 128
TT = 512
EPS = 1e-8
BF = ml_dtypes.bfloat16


class Tok:
    __slots__ = ("eng", "seq", "sem", "val")
    def __init__(self, eng, seq, sem, val):
        self.eng, self.seq, self.sem, self.val = eng, seq, sem, val


class Res:
    __slots__ = ("writer", "readers", "name")
    def __init__(self, name=""):
        self.writer = None
        self.readers = []
        self.name = name


class Prog:
    ENGS = ("tensor", "vector", "scalar", "gpsimd", "sync")

    def __init__(self, nc, sems):
        self.nc = nc
        self.ops = {e: [] for e in self.ENGS}
        self.scnt = {}
        self.watermark = {e: {} for e in self.ENGS}
        self.engsem = sems

    def raw_emit(self, eng, fn, deps, sem, inc, tag):
        waits = {}
        for d in deps:
            if d is None or d.eng == eng:
                continue
            sid = id(d.sem)
            if waits.get(sid, (None, 0))[1] < d.val:
                waits[sid] = (d.sem, d.val)
        todo = []
        wm = self.watermark[eng]
        for sid, (s, v) in waits.items():
            if wm.get(sid, 0) < v:
                todo.append((s, v))
                wm[sid] = v
        if inc:
            self.scnt[id(sem)] = self.scnt.get(id(sem), 0) + inc
            tok = Tok(eng, len(self.ops[eng]), sem, self.scnt[id(sem)])
        else:
            tok = Tok(eng, len(self.ops[eng]), None, None)
        self.ops[eng].append((todo, fn, sem if inc else None, inc, tag))
        return tok

    def finalize(self, block, tail=None):
        def make_body(eng_name):
            def body(eng):
                for todo, fn, sem, inc, tag in self.ops[eng_name]:
                    for (s, v) in todo:
                        eng.wait_ge(s, v)
                    ins = fn(eng)
                    if inc:
                        ins.then_inc(sem, inc)
                if tail and eng_name in tail:
                    tail[eng_name](eng)
            return body
        for e in ("sync", "vector", "scalar", "tensor", "gpsimd"):
            getattr(block, e)(make_body(e))


def host_consts():
    s8 = np.zeros((8, 1024), np.float32)
    for h in range(8):
        s8[h, h * 128:(h + 1) * 128] = 1.0
    tribig = np.where(np.arange(128)[None, :] > np.arange(128)[:, None], -1e30, 0.0).astype(np.float32)
    itile = np.tile(np.eye(128, dtype=np.float32), (1, 8))
    ones = np.ones((128, 128), np.float32)
    onesb = np.ones((128, 1), BF)
    i128b = np.eye(128, dtype=np.float32).astype(BF)
    maskb = np.zeros((128, 32), np.float32)
    maska = np.zeros((128, 32), np.float32)
    for kc in range(4):
        for r in range(128):
            h = (kc * 128 + r) // 64
            maskb[r, kc * 8 + h] = 0.125
            maska[r, kc * 8 + h] = -1.0
    cvals = np.array([-0.5, 1e-5, -1.0, 1.0 / 1024.0, 0.0], np.float32)
    cst = np.tile(cvals[None, :], (128, 1)).astype(np.float32)
    return dict(c_s8=s8, c_tribig=tribig, c_itile=itile, c_ones=ones,
                c_onesb=onesb, c_i128b=i128b, c_maskb=maskb.astype(BF),
                c_maska=maska.astype(BF), c_cst=cst)


def build_core(Bc=2, S=2048, stage=2, sections="abcdefgh12345"):
    NCH = S // C
    NT = S // TT
    CPT = TT // C
    NSUB = TT // 128

    nc = bass.Bass("TRN2", target_bir_lowering=False, detect_race_conditions=False)
    x_in = nc.dram_tensor("x_in", [Bc, S, D], F32, kind="ExternalInput").ap()
    wq_in = nc.dram_tensor("wq_in", [512, 1024], F32, kind="ExternalInput").ap()
    wk_in = nc.dram_tensor("wk_in", [512, 1024], F32, kind="ExternalInput").ap()
    wal_in = nc.dram_tensor("wal_in", [512, 1024], F32, kind="ExternalInput").ap()
    wv_in = nc.dram_tensor("wv_in", [1024, 1024], F32, kind="ExternalInput").ap()
    wo_in = nc.dram_tensor("wo_in", [1024, 1024], F32, kind="ExternalInput").ap()
    walb_in = nc.dram_tensor("walb_in", [128, 4], F32, kind="ExternalInput").ap()
    wg_in = nc.dram_tensor("wg_in", [I, 1024], F32, kind="ExternalInput").ap()
    wu_in = nc.dram_tensor("wu_in", [I, 1024], F32, kind="ExternalInput").ap()
    wd_in = nc.dram_tensor("wd_in", [1024, I], F32, kind="ExternalInput").ap()
    hc = host_consts()
    cin = {}
    for k, v in hc.items():
        dt = BF16 if v.dtype == BF else F32
        cin[k] = nc.dram_tensor(k, list(v.shape), dt, kind="ExternalInput").ap()
    out_d = nc.dram_tensor("out", [Bc, S, D], F32, kind="ExternalOutput").ap()
    x2_d = nc.dram_tensor("x2_d", [Bc * S, D], F32, kind="Internal").ap()
    h2qt_d = nc.dram_tensor("h2qt_d", [D, Bc * S], BF16, kind="Internal").ap()
    wqgt_d = nc.dram_tensor("wqgt_d", [1024, I], BF16, kind="Internal").ap()
    wqut_d = nc.dram_tensor("wqut_d", [1024, I], BF16, kind="Internal").ap()
    wqdt_d = nc.dram_tensor("wqdt_d", [I, 1024], BF16, kind="Internal").ap()

    with ExitStack() as g:
        eg = g.enter_context
        s_pe = eg(nc.semaphore("s_pe"))
        s_dve = eg(nc.semaphore("s_dve"))
        s_act = eg(nc.semaphore("s_act"))
        s_gp = eg(nc.semaphore("s_gp"))
        s_const = eg(nc.semaphore("s_const"))
        s_w = eg(nc.semaphore("s_w"))
        s_x = [eg(nc.semaphore(f"s_x{i}")) for i in range(2)]
        s_xr = [eg(nc.semaphore(f"s_xr{i}")) for i in range(2)]
        s_cat = eg(nc.semaphore("s_cat"))
        s_c = eg(nc.semaphore("s_c"))
        s_iz = eg(nc.semaphore("s_iz"))
        s_scr = eg(nc.semaphore("s_scr"))
        s_sx2 = eg(nc.semaphore("s_sx2"))
        s_sh2 = eg(nc.semaphore("s_sh2"))
        s_fl = [eg(nc.semaphore(f"s_fl{i}")) for i in range(2)]
        s_out = eg(nc.semaphore("s_out"))

        P = Prog(nc, {"tensor": s_pe, "vector": s_dve, "scalar": s_act,
                      "gpsimd": s_gp, "sync": s_const})

        def em(eng, fn, reads=(), writes=(), extra=(), sem=None, inc=1, tag=""):
            deps = list(extra)
            for r in reads:
                deps.append(r.writer)
            for w in writes:
                deps.append(w.writer)
                deps.extend(w.readers)
            tok = P.raw_emit(eng, fn, deps, sem or P.engsem[eng], inc, tag)
            for r in reads:
                r.readers.append(tok)
            for w in writes:
                w.writer = tok
                w.readers = []
            return tok

        def mmg(parts, reads=(), writes=(), extra=(), tag="mm"):
            deps = list(extra)
            for r in reads:
                deps.append(r.writer)
            for w in writes:
                deps.append(w.writer)
                deps.extend(w.readers)
            tok = None
            n = len(parts)
            for i, fn in enumerate(parts):
                tok = P.raw_emit("tensor", fn, deps if i == 0 else (), s_pe,
                                 1 if i == n - 1 else 0, tag)
            for r in reads:
                r.readers.append(tok)
            for w in writes:
                w.writer = tok
                w.readers = []
            return tok

        pA = eg(nc.psum_tensor("pA", [128, 1024], F32))
        pB = eg(nc.psum_tensor("pB", [128, 1024], F32))
        pC = eg(nc.psum_tensor("pC", [128, 1024], F32))
        pN = eg(nc.psum_tensor("pN", [8, 1024], F32))
        rA = [Res("pA0"), Res("pA1")]
        rB = [Res("pB0"), Res("pB1")]
        rC = [Res("pC0"), Res("pC1")]
        rN = [Res("pN0"), Res("pN1")]

        s8 = eg(nc.sbuf_tensor("s8", [8, 1024], F32))
        tribig = eg(nc.sbuf_tensor("tribig", [128, 128], F32))
        itile = eg(nc.sbuf_tensor("itile", [128, 1024], F32))
        ones = eg(nc.sbuf_tensor("ones", [128, 128], F32))
        onesb = eg(nc.sbuf_tensor("onesb", [128, 1], BF16))
        i128b = eg(nc.sbuf_tensor("i128b", [128, 128], BF16))
        maskb = eg(nc.sbuf_tensor("maskb", [128, 32], BF16))
        maska = eg(nc.sbuf_tensor("maska", [128, 32], BF16))
        cst = eg(nc.sbuf_tensor("cst", [128, 8], F32))
        dummy = eg(nc.sbuf_tensor("barbuf", [1, 16], F32))
        r_dummy = Res("dummy")

        def engine_barrier(tagb):
            tv = em("vector", lambda e: e.memset(dummy[:, 0:8], 0.0), writes=[r_dummy], tag=tagb)
            ta = em("scalar", lambda e: e.activation(dummy[:, 8:16], dummy[:, 0:8], AF.Identity, bias=0.0, scale=1.0),
                    reads=[r_dummy], tag=tagb)
            tp = mmg([(lambda e: e.matmul(pN[0:1, 0:16], ones[0:1, 0:1], ones[0:1, 0:16], start=True, stop=True))],
                     reads=[r_const], writes=[rN[0]], tag=tagb)
            return [tv, ta, tp]
        walb = eg(nc.sbuf_tensor("walb", [128, 4], F32))
        CN = {"c_s8": s8, "c_tribig": tribig, "c_itile": itile, "c_ones": ones,
              "c_onesb": onesb, "c_i128b": i128b, "c_maskb": maskb, "c_maska": maska}
        r_qs = Res("qscratch")

        def alloc_qscr(ee, KW, pfx, alias_k=True):
            d_u = ee(nc.sbuf_tensor(pfx + "q_u", [128, KW], BF16))
            return dict(
                q_absf=ee(nc.sbuf_tensor(pfx + "q_absf", [128, KW], F32)),
                q_u=d_u,
                q_h=ee(nc.sbuf_tensor(pfx + "q_h", [128, KW], BF16)),
                q_k=d_u if alias_k else ee(nc.sbuf_tensor(pfx + "q_k", [128, KW], BF16)),
                q_sgn=ee(nc.sbuf_tensor(pfx + "q_sgn", [128, KW], BF16)),
                q_scl=ee(nc.sbuf_tensor(pfx + "q_scl", [128, KW // 16], F32)),
                q_inv=ee(nc.sbuf_tensor(pfx + "q_inv", [128, KW // 16], F32)),
                q_sclb=ee(nc.sbuf_tensor(pfx + "q_sclb", [128, KW // 16], BF16)),
            )

        r_wqt, r_wkt, r_walt, r_wvt, r_wot = (Res(x) for x in ("wqt", "wkt", "walt", "wvt", "wot"))
        r_const = Res("const")
        r_x2d = Res("x2_d")
        r_h2d = Res("h2qt_d")
        r_wqd_scr = Res("wq_scr")

        for k in CN:
            em("sync", lambda e, t=CN[k], s=cin[k]: e.dma_start(out=t[:], in_=s),
               writes=[r_const] if False else (), sem=s_const, inc=16, tag="ldc")
        em("sync", lambda e: e.dma_start(out=cst[:, 0:5], in_=cin["c_cst"]), sem=s_const, inc=16, tag="ldc")
        t_ld = em("sync", lambda e: e.dma_start(out=walb[:], in_=walb_in),
                  writes=[r_const], sem=s_const, inc=16, tag="ldc")
        c_neghalf = cst[:, 0:1]
        c_lneps = cst[:, 1:2]
        c_negone = cst[:, 2:3]

        def quant(x_bf, K, out_bf, r_in, r_out, qs, rqs=None, kf32=True, tag="q"):
            if rqs is None:
                rqs = r_qs
            nb = K // 16
            q_absf, q_u, q_h, q_k, q_sgn, q_scl, q_inv, q_sclb = (
                qs["q_absf"], qs["q_u"], qs["q_h"], qs["q_k"], qs["q_sgn"],
                qs["q_scl"], qs["q_inv"], qs["q_sclb"])
            em("scalar", lambda e: e.activation(q_absf[:, 0:K], x_bf, AF.Abs, bias=0.0, scale=1.0),
               reads=[r_in], writes=[rqs], tag=tag + "abs")
            em("scalar", lambda e: e.activation(q_sgn[:, 0:K], x_bf, AF.Sign, bias=0.0, scale=1.0),
               reads=[r_in, rqs], tag=tag + "sgn")
            av = q_absf[:, 0:K].rearrange("p (n k) -> p n k", k=16)
            em("vector", lambda e: e.tensor_reduce(q_scl[:, 0:nb].unsqueeze(2), av, AX.X, ALU.max,
                                                   apply_absolute_value=True), reads=[rqs], tag=tag + "red")
            em("vector", lambda e: e.tensor_scalar(q_scl[:, 0:nb], q_scl[:, 0:nb], 1.0 / 6.0, 1e-12, ALU.mult, ALU.max),
               reads=[rqs], tag=tag)
            em("vector", lambda e: e.reciprocal(q_inv[:, 0:nb], q_scl[:, 0:nb]), reads=[rqs], tag=tag)
            em("vector", lambda e: e.tensor_scalar(q_sclb[:, 0:nb], q_scl[:, 0:nb], 1.0, None, ALU.mult),
               reads=[rqs], tag=tag)
            iv = q_inv[:, 0:nb].unsqueeze(2).broadcast_to([128, nb, 16])
            em("vector", lambda e: e.tensor_tensor(av, av, iv, ALU.mult), reads=[rqs], tag=tag + "u")
            uf = q_absf[:, 0:K]
            em("vector", lambda e: e.tensor_scalar(q_u[:, 0:K], uf, 1.0, None, ALU.mult), reads=[rqs], tag=tag + "ub")
            u = q_u[:, 0:K]
            em("vector", lambda e: e.tensor_scalar(q_h[:, 0:K], u, 0.25, None, ALU.is_gt), reads=[rqs], tag=tag)
            for th in (0.75, 1.25, 1.75):
                em("vector", lambda e, th=th: e.scalar_tensor_tensor(q_h[:, 0:K], u, th, q_h[:, 0:K], ALU.is_gt, ALU.add),
                   reads=[rqs], tag=tag)
            uk = uf if kf32 else u
            em("vector", lambda e, uk=uk: e.tensor_scalar(q_k[:, 0:K], uk, 5.0, None, ALU.is_gt), reads=[rqs], tag=tag)
            for th in (5.0, 3.5, 2.5):
                em("vector", lambda e, th=th, uk=uk: e.scalar_tensor_tensor(q_k[:, 0:K], uk, th, q_k[:, 0:K], ALU.is_gt, ALU.add),
                   reads=[rqs], tag=tag)
            em("vector", lambda e: e.scalar_tensor_tensor(q_h[:, 0:K], q_h[:, 0:K], 0.5, q_k[:, 0:K], ALU.mult, ALU.add),
               reads=[rqs], tag=tag)
            em("vector", lambda e: e.tensor_mul(q_h[:, 0:K], q_h[:, 0:K], q_sgn[:, 0:K]), reads=[rqs], tag=tag)
            sb = q_sclb[:, 0:nb].unsqueeze(2).broadcast_to([128, nb, 16])
            t_out = em("vector", lambda e: e.tensor_tensor(
                out_bf.rearrange("p (n k) -> p n k", k=16),
                q_h[:, 0:K].rearrange("p (n k) -> p n k", k=16), sb, ALU.mult),
                reads=[rqs], writes=[r_out], tag=tag + "out")
            return t_out

        def transpose_tiles(srcs, psum_ap, psum_res, dst_view, r_src, r_dst, extra=(), tag="tr"):
            n = len(srcs)
            parts = [(lambda e, i=i, s=srcs[i]: e.matmul(psum_ap[:, i * 128:(i + 1) * 128], s, i128b[:],
                                                          start=True, stop=True)) for i in range(n)]
            mm_t = mmg(parts, reads=[r_src, r_const], writes=[psum_res], extra=extra, tag=tag)
            return em("scalar", lambda e: e.activation(
                dst_view, psum_ap[:, 0:n * 128], AF.Copy, bias=0.0, scale=1.0),
                reads=[psum_res], writes=[r_dst], tag=tag + "ev")

        ph0A = ExitStack()
        e0A = ph0A.enter_context
        wqt = e0A(nc.sbuf_tensor("wqt", [128, 8 * 512], BF16))
        wkt = e0A(nc.sbuf_tensor("wkt", [128, 8 * 512], BF16))
        walt = e0A(nc.sbuf_tensor("walt", [128, 8 * 512], BF16))
        wvt = e0A(nc.sbuf_tensor("wvt", [128, 8 * 1024], BF16))
        wot = e0A(nc.sbuf_tensor("wot", [128, 8 * 1024], BF16))
        with ExitStack() as ph0:
            e0 = ph0.enter_context
            wld = e0(nc.sbuf_tensor("wld", [128, 8192], F32))
            wldb = e0(nc.sbuf_tensor("wldb", [128, 8192], BF16))
            r_wld, r_wldb = Res("wld"), Res("wldb")

            def prep_attn_w(src, rows, dstT, r_dst, ncols, tag):
                nrt = rows // 128
                em("sync", lambda e: e.dma_start(
                    out=wld[:, 0:nrt * 1024].rearrange("p (rt k) -> p rt k", k=1024),
                    in_=src.rearrange("(rt p) k -> p rt k", p=128)),
                   writes=[r_wld], sem=s_w, inc=16, tag=tag + "ld")
                em("scalar", lambda e: e.activation(wldb[:, 0:nrt * 1024], wld[:, 0:nrt * 1024],
                                                    AF.Copy, bias=0.0, scale=1.0),
                   reads=[r_wld], writes=[r_wldb], extra=[t_ld], tag=tag + "cast")
                for rt in range(nrt):
                    for g2 in range(2):
                        srcs = [wldb[:, rt * 1024 + (g2 * 4 + q2) * 128: rt * 1024 + (g2 * 4 + q2 + 1) * 128]
                                for q2 in range(4)]
                        dv = dstT[:].rearrange("p (kc n) -> p kc n", n=ncols)[:, g2 * 4:(g2 + 1) * 4,
                                                                             rt * 128:(rt + 1) * 128]
                        pi = (rt * 2 + g2) % 2
                        transpose_tiles(srcs, pC[:, pi * 512:(pi + 1) * 512], rC[pi], dv,
                                        r_wldb, r_dst, tag=tag + "tr")

            prep_attn_w(wq_in, 512, wqt, r_wqt, 512, "wq")
            prep_attn_w(wk_in, 512, wkt, r_wkt, 512, "wk")
            prep_attn_w(wal_in, 512, walt, r_walt, 512, "wal")
            prep_attn_w(wv_in, 1024, wvt, r_wvt, 1024, "wv")
            prep_attn_w(wo_in, 1024, wot, r_wot, 1024, "wo")

        barW = engine_barrier("barW")
        with ExitStack() as ph0f:
            e0 = ph0f.enter_context
            wldF = [e0(nc.sbuf_tensor(f"wldF{i}", [128, I], F32)) for i in range(2)]
            r_wldF = [Res("wldF0"), Res("wldF1")]
            s_wf = [eg(nc.semaphore(f"s_wf{i}")) for i in range(2)]
            wqtm2 = [e0(nc.sbuf_tensor(f"wqtm{i}", [128, I], BF16)) for i in range(2)]
            wqfm2 = [e0(nc.sbuf_tensor(f"wqfm{i}", [128, I], BF16)) for i in range(2)]
            r_wqtm2 = [Res("wqtm0"), Res("wqtm1")]
            r_wqfm2 = [Res("wqfm0"), Res("wqfm1")]
            qs02 = [alloc_qscr(e0, I, f"p0q{i}", alias_k=False) for i in range(2)]
            rqs02 = [Res("qs0a"), Res("qs0b")]

            def prep_ffn_w(src, rows, K, dst_d, rti0, tag, kf32=True):
                nrt = rows // 128
                nkc = K // 128
                for rt in range(nrt):
                    pi = (rti0 + rt) % 2
                    em("sync", lambda e, rt=rt, pi=pi: e.dma_start(out=wldF[pi][:, 0:K], in_=src[rt * 128:(rt + 1) * 128, :]),
                       writes=[r_wldF[pi]], extra=barW, sem=s_wf[pi], inc=16, tag=tag + "ld")
                    quant(wldF[pi][:, 0:K], K, wqtm2[pi][:, 0:K], r_wldF[pi], r_wqtm2[pi],
                          qs02[pi], rqs02[pi], kf32=kf32, tag=tag + "q")
                    for gq in range(0, nkc, 4):
                        ng = min(4, nkc - gq)
                        srcs = [wqtm2[pi][:, (gq + q2) * 128:(gq + q2 + 1) * 128] for q2 in range(ng)]
                        dv = wqfm2[pi][:, gq * 128:(gq + ng) * 128]
                        transpose_tiles(srcs, pC[:, (pi) * 512:(pi + 1) * 512], rC[pi], dv,
                                        r_wqtm2[pi], r_wqfm2[pi], tag=tag + "tr")
                    em("sync", lambda e, rt=rt, pi=pi: e.dma_start(
                        out=dst_d.rearrange("(kc p) r -> p kc r", p=128)[:, :, rt * 128:(rt + 1) * 128],
                        in_=wqfm2[pi][:, 0:nkc * 128].rearrange("p (kc r) -> p kc r", r=128)),
                        reads=[r_wqfm2[pi]], writes=[r_wqd_scr], sem=s_scr, inc=16, tag=tag + "st")
                return rti0 + nrt

            rti = prep_ffn_w(wg_in, I, 1024, wqgt_d, 0, "wg", kf32=False)
            rti = prep_ffn_w(wu_in, I, 1024, wqut_d, rti, "wu", kf32=False)
            rti = prep_ffn_w(wd_in, 1024, I, wqdt_d, rti, "wd", kf32=True)

        bar0 = engine_barrier("bar0")
        bar0.append(r_wqd_scr.writer)
        with ExitStack() as phA:
            eA = phA.enter_context
            xt = [eA(nc.sbuf_tensor(f"xt{i}", [128, 1024], F32)) for i in range(2)]
            xrt = [eA(nc.sbuf_tensor(f"xrt{i}", [128, 1024], F32)) for i in range(2)]
            r_xt = [Res("xt0"), Res("xt1")]
            r_xrt = [Res("xrt0"), Res("xrt1")]
            qsA = alloc_qscr(eA, 1024, "pA")
            junk = qsA["q_absf"]
            lnt = eA(nc.sbuf_tensor("lnt", [128, 8], F32))
            r_lnt = Res("lnt")
            xh_tm = eA(nc.sbuf_tensor("xh_tm", [128, 4 * 1024], BF16))
            xh_fm = eA(nc.sbuf_tensor("xh_fm", [128, 8 * 512], BF16))
            q_fm = eA(nc.sbuf_tensor("q_fm", [128, 4 * 512], BF16))
            qk_fm = eA(nc.sbuf_tensor("qk_fm", [128, 4 * 512], BF16))
            sp_fm = eA(nc.sbuf_tensor("sp_fm", [128, 4 * 512], BF16))
            al_e = eA(nc.sbuf_tensor("al_e", [128, 512], BF16))
            al_l = eA(nc.sbuf_tensor("al_l", [128, 512], BF16))
            al_r = eA(nc.sbuf_tensor("al_r", [128, 512], BF16))
            sgn_v = eA(nc.sbuf_tensor("sgn_v", [128, 512], BF16))
            av_v = qsA["q_absf"]
            vt_t = eA(nc.sbuf_tensor("vt_t", [128, 4 * 1024], BF16))
            a_hm = eA(nc.sbuf_tensor("a_hm", [8, TT], F32))
            b_hm = eA(nc.sbuf_tensor("b_hm", [8, TT], F32))
            y_tm = eA(nc.sbuf_tensor("y_tm", [128, 1024], BF16))
            y_fm = eA(nc.sbuf_tensor("y_fm", [128, 4 * 1024], BF16))
            x2sb = eA(nc.sbuf_tensor("x2sb", [128, 1024], F32))
            h2sb = eA(nc.sbuf_tensor("h2sb", [128, 1024], F32))
            h2q2 = [eA(nc.sbuf_tensor(f"h2q{i}", [128, 1024], BF16)) for i in range(2)]
            h2qf = eA(nc.sbuf_tensor("h2qf", [128, 1024], BF16))
            (r_xhtm, r_xhfm, r_qfm, r_qkfm, r_spfm, r_ale, r_all, r_alr, r_sgnv,
             r_vt, r_ahm, r_bhm, r_ytm, r_yfm, r_x2, r_h2,
             r_h2qf) = (Res(x) for x in
                        ("xh_tm", "xh_fm", "q_fm", "qk_fm", "sp_fm", "al_e", "al_l", "al_r",
                         "sgn_v", "vt_t", "a_hm", "b_hm", "y_tm", "y_fm", "x2",
                         "h2", "h2qf"))
            r_h2q2 = [Res("h2q0"), Res("h2q1")]
            lam = eA(nc.sbuf_tensor("lam", [8, 1], F32))
            Rt = eA(nc.sbuf_tensor("Rt", [8, D], F32))
            Rb = eA(nc.sbuf_tensor("Rb", [8, D], BF16))
            r_cs = eA(nc.sbuf_tensor("r_cs", [8, C], F32))
            w_hm = eA(nc.sbuf_tensor("w_hm", [8, C], F32))
            M_hm = eA(nc.sbuf_tensor("M_hm", [8, C], F32))
            negM = eA(nc.sbuf_tensor("negM", [8, C], F32))
            negM_cat = eA(nc.sbuf_tensor("negM_cat", [1, 1024], F32))
            Emm = eA(nc.sbuf_tensor("Emm", [128, 1024], BF16))
            Ett = qsA["q_u"]
            zc_row = eA(nc.sbuf_tensor("zc_row", [1, 1024], F32))
            zrow = eA(nc.sbuf_tensor("zrow", [1, 1024], F32))
            ciz = eA(nc.sbuf_tensor("ciz", [1, 2048], F32))
            c_row = ciz[0:1, 0:1024]
            izrow = ciz[0:1, 1024:2048]
            cizhm = eA(nc.sbuf_tensor("cizhm", [8, 256], F32))
            c_hm = cizhm[:, 0:128]
            iz_hm = cizhm[:, 128:256]
            c_hmb = eA(nc.sbuf_tensor("c_hmb", [8, C], BF16))
            wcol = eA(nc.sbuf_tensor("wcol", [128, 512], BF16))
            wtot = eA(nc.sbuf_tensor("wtot", [128, 128], BF16))
            lnzt = eA(nc.sbuf_tensor("lnzt", [8, 1], F32))
            (r_lam, r_R, r_Rb, r_rcs, r_whm, r_Mhm, r_negM, r_cat, r_Emm, r_Et,
             r_zc, r_z, r_iz, r_cr, r_chm, r_chmb, r_izhm, r_wcol, r_wtot,
             r_lnz, r_tsc) = (Res(x) for x in
                              ("lam", "R", "Rb", "rcs", "whm", "Mhm", "negM", "cat", "Emm",
                               "Et", "zc", "z", "iz", "cr", "chm", "chmb", "izhm", "wcol",
                               "wtot", "lnz", "tsc"))

            SX, SSQ, MEAN, VAR, RSTD, NMR = (lnt[:, i:i + 1] for i in range(6))

            def ln_emit(src_ap, r_src, dst_ap, r_dst, tag):
                em("vector", lambda e: e.tensor_scalar(junk[:], src_ap, 0.0, 0.0, ALU.add, ALU.add,
                                                       accum_out=SX), reads=[r_src], writes=[r_qs, r_lnt], tag=tag)
                em("vector", lambda e: e.scalar_tensor_tensor(junk[:], src_ap, 1.0, src_ap, ALU.bypass, ALU.mult,
                                                              accum_out=SSQ), reads=[r_src], writes=[r_qs], tag=tag)
                em("vector", lambda e: e.tensor_scalar(MEAN, SX, 1.0 / D, None, ALU.mult), tag=tag)
                em("vector", lambda e: e.tensor_mul(NMR, MEAN, MEAN), tag=tag)
                em("vector", lambda e: e.scalar_tensor_tensor(VAR, SSQ, 1.0 / D, NMR, ALU.mult, ALU.subtract),
                   writes=[r_lnt], tag=tag)
                em("scalar", lambda e: e.activation(VAR, VAR, AF.Ln, bias=c_lneps, scale=1.0),
                   reads=[r_lnt], extra=[t_ld], tag=tag)
                em("scalar", lambda e: e.activation(RSTD, VAR, AF.Exp, bias=0.0, scale=c_neghalf),
                   writes=[r_lnt], tag=tag)
                em("vector", lambda e: e.scalar_tensor_tensor(NMR, MEAN, -1.0, RSTD, ALU.mult, ALU.mult),
                   reads=[r_lnt], writes=[r_lnt], tag=tag)
                return em("scalar", lambda e: e.activation(dst_ap, src_ap, AF.Identity, bias=NMR, scale=RSTD),
                          reads=[r_src, r_lnt], writes=[r_dst], tag=tag + "n")

            pendA = None

            def h2_tail(b, t0, sub):
                h2q = h2q2[sub % 2]
                for g2 in range(2):
                    srcs = [h2q[:, (g2 * 4 + q2) * 128:(g2 * 4 + q2 + 1) * 128] for q2 in range(4)]
                    transpose_tiles(srcs, pC[:, g2 * 512:(g2 + 1) * 512], rC[g2],
                                    h2qf[:, g2 * 512:(g2 + 1) * 512],
                                    r_h2q2[sub % 2], r_h2qf, tag="h2T")
                em("sync", lambda e, b=b, t0=t0, sub=sub: e.dma_start(
                    out=h2qt_d.rearrange("(kc p) t -> p kc t", p=128)[:, :, b * S + t0 + sub * 128: b * S + t0 + (sub + 1) * 128],
                    in_=h2qf[:].rearrange("p (kc r) -> p kc r", r=128)),
                    reads=[r_h2qf], writes=[r_h2d], sem=s_sh2, inc=16, tag="h2st")

            em("vector", lambda e: e.memset(lam[:], -1e30), writes=[r_lam], extra=[t_ld], tag="init")
            em("vector", lambda e: e.memset(Rt[:], 0.0), writes=[r_R], tag="init")
            em("vector", lambda e: e.memset(Rb[:], 0.0), writes=[r_Rb], tag="init")

            for b in range(Bc if stage >= 1 else 0):
                if b > 0:
                    em("vector", lambda e: e.memset(lam[:], -1e30), writes=[r_lam], tag="reinit")
                    em("vector", lambda e: e.memset(Rt[:], 0.0), writes=[r_R], tag="reinit")
                    em("vector", lambda e: e.memset(Rb[:], 0.0), writes=[r_Rb], tag="reinit")
                for T in range(NT):
                    t0 = T * TT
                    for sub in range(NSUB if "a" in sections else 0):
                        ri = sub % 2
                        em("sync", lambda e, b=b, t0=t0, sub=sub, ri=ri: e.dma_start(
                            out=xt[ri][:], in_=x_in[b, t0 + sub * 128:t0 + (sub + 1) * 128, :]),
                            writes=[r_xt[ri]], extra=bar0, sem=s_x[ri], inc=16, tag="xld")
                        ln_emit(xt[ri][:], r_xt[ri],
                                xh_tm[:, sub * 1024:(sub + 1) * 1024], r_xhtm, f"ln1")
                    for sub in range(NSUB if "b" in sections else 0):
                        for g2 in range(2):
                            srcs = [xh_tm[:, sub * 1024 + (g2 * 4 + q2) * 128: sub * 1024 + (g2 * 4 + q2 + 1) * 128]
                                    for q2 in range(4)]
                            dv = xh_fm[:].rearrange("p (kc t) -> p kc t", t=512)[:, g2 * 4:(g2 + 1) * 4,
                                                                                sub * 128:(sub + 1) * 128]
                            transpose_tiles(srcs, pB[:, g2 * 512:(g2 + 1) * 512], rB[g2], dv,
                                            r_xhtm, r_xhfm, tag="xhT")
                    for ncnk in range(4 if "c" in sections else 0):
                        hf = ncnk % 2
                        psA = pA[:, hf * 512:(hf + 1) * 512]
                        mmg([(lambda e, kc=kc, ncnk=ncnk, psA=psA: e.matmul(
                            psA, wqt[:, kc * 512 + ncnk * 128: kc * 512 + (ncnk + 1) * 128],
                            xh_fm[:, kc * 512:(kc + 1) * 512], start=(kc == 0), stop=(kc == 7)))
                            for kc in range(8)], reads=[r_wqt, r_xhfm], writes=[rA[hf]], tag="qmm")
                        em("scalar", lambda e, ncnk=ncnk, psA=psA: e.activation(
                            q_fm[:, ncnk * 512:(ncnk + 1) * 512], psA, AF.Copy, bias=0.0, scale=1.0),
                            reads=[rA[hf]], writes=[r_qfm], tag="qev")
                        mmg([(lambda e, kc=kc, ncnk=ncnk, psA=psA: e.matmul(
                            psA, wkt[:, kc * 512 + ncnk * 128: kc * 512 + (ncnk + 1) * 128],
                            xh_fm[:, kc * 512:(kc + 1) * 512], start=(kc == 0), stop=(kc == 7)))
                            for kc in range(8)], reads=[r_wkt, r_xhfm], writes=[rA[hf]], tag="kmm")
                        em("vector", lambda e, ncnk=ncnk, psA=psA: e.tensor_tensor(
                            qk_fm[:, ncnk * 512:(ncnk + 1) * 512], q_fm[:, ncnk * 512:(ncnk + 1) * 512],
                            psA, ALU.mult), reads=[rA[hf], r_qfm], writes=[r_qkfm], tag="qk")
                        if "g" not in sections:
                            continue
                        psB = pB[:, hf * 512:(hf + 1) * 512]
                        mmg([(lambda e, kc=kc, ncnk=ncnk, psB=psB: e.matmul(
                            psB, walt[:, kc * 512 + ncnk * 128: kc * 512 + (ncnk + 1) * 128],
                            xh_fm[:, kc * 512:(kc + 1) * 512], start=(kc == 0), stop=(kc == 7)))
                            for kc in range(8)], reads=[r_walt, r_xhfm], writes=[rB[hf]], tag="almm")
                        balc = walb[:, ncnk:ncnk + 1]
                        if "1" in sections:
                            em("scalar", lambda e, psB=psB, balc=balc: e.activation(al_e[:], psB, AF.Abs, bias=balc, scale=1.0),
                               reads=[rB[hf], r_const], writes=[r_ale], tag="spabs")
                        if "2" in sections:
                            em("scalar", lambda e: e.activation(al_e[:], al_e[:], AF.Exp, bias=0.0, scale=c_negone),
                               reads=[r_ale], writes=[r_ale], tag="spexp")
                        if "3" in sections:
                            em("scalar", lambda e: e.activation(al_l[:], al_e[:], AF.Ln, bias=1.0, scale=1.0),
                               reads=[r_ale], writes=[r_all], tag="spln")
                        if "4" in sections:
                            em("scalar", lambda e, psB=psB, balc=balc: e.activation(al_r[:], psB, AF.Relu, bias=balc, scale=1.0),
                               reads=[rB[hf], r_const], writes=[r_alr], tag="sprelu")
                        if "5" in sections:
                            em("vector", lambda e, ncnk=ncnk: e.tensor_add(
                                sp_fm[:, ncnk * 512:(ncnk + 1) * 512], al_l[:], al_r[:]),
                                reads=[r_all, r_alr], writes=[r_spfm], tag="sp")
                    if "h" not in sections:
                        continue
                    mmg([(lambda e, kc=kc: e.matmul(pN[:, 0:512], maskb[:, kc * 8:(kc + 1) * 8],
                                                    qk_fm[:, kc * 512:(kc + 1) * 512],
                                                    start=(kc == 0), stop=(kc == 3))) for kc in range(4)],
                        reads=[r_const, r_qkfm], writes=[rN[0]], tag="bmm")
                    mmg([(lambda e, kc=kc: e.matmul(pN[:, 512:1024], maska[:, kc * 8:(kc + 1) * 8],
                                                    sp_fm[:, kc * 512:(kc + 1) * 512],
                                                    start=(kc == 0), stop=(kc == 3))) for kc in range(4)],
                        reads=[r_const, r_spfm], writes=[rN[1]], tag="amm")
                    em("vector", lambda e: e.tensor_copy(b_hm[:], pN[:, 0:512]),
                       reads=[rN[0]], writes=[r_bhm], tag="bev")
                    em("vector", lambda e: e.tensor_copy(a_hm[:], pN[:, 512:1024]),
                       reads=[rN[1]], writes=[r_ahm], tag="aev")
                    if T == 0:
                        em("vector", lambda e: e.memset(a_hm[:, 0:1], 0.0), writes=[r_ahm], tag="a0")
                    for sub in range(NSUB if "d" in sections else 0):
                        for hf in range(2):
                            psC = pC[:, hf * 512:(hf + 1) * 512]
                            mmg([(lambda e, kc=kc, sub=sub, hf=hf, psC=psC: e.matmul(
                                psC, xh_fm[:, kc * 512 + sub * 128: kc * 512 + (sub + 1) * 128],
                                wvt[:, kc * 1024 + hf * 512: kc * 1024 + (hf + 1) * 512],
                                start=(kc == 0), stop=(kc == 7))) for kc in range(8)],
                                reads=[r_xhfm, r_wvt], writes=[rC[hf]], tag="vmm")
                            em("scalar", lambda e, psC=psC: e.activation(sgn_v[:], psC, AF.Sign, bias=0.0, scale=1.0),
                               reads=[rC[hf]], writes=[r_sgnv], tag="vsgn")
                            em("scalar", lambda e, psC=psC: e.activation(av_v[:, 0:512], psC, AF.Abs, bias=0.0, scale=1.0),
                               reads=[rC[hf]], writes=[r_qs], tag="vabs")
                            em("vector", lambda e: e.tensor_scalar(av_v[:, 0:512], av_v[:, 0:512], 1.0 + EPS, EPS * (1.0 + EPS),
                                                                   ALU.mult, ALU.add),
                               reads=[r_qs], writes=[r_qs], tag="veps")
                            if T == 0 and sub == 0:
                                em("vector", lambda e: e.tensor_scalar(av_v[0:1, 0:512], av_v[0:1, 0:512],
                                                                       1.0 / (1.0 + EPS), 0.0, ALU.mult, ALU.add),
                                   reads=[r_qs], writes=[r_qs], tag="veps0")
                            em("vector", lambda e, sub=sub, hf=hf: e.tensor_mul(
                                vt_t[:, sub * 1024 + hf * 512: sub * 1024 + (hf + 1) * 512], av_v[:, 0:512], sgn_v[:]),
                                reads=[r_qs, r_sgnv], writes=[r_vt], tag="vt")
                    for jj in range(CPT if "e" in sections else 0):
                        sl = slice(jj * C, (jj + 1) * C)
                        em("vector", lambda e, sl=sl: e.tensor_tensor_scan(r_cs[:], a_hm[:, sl], a_hm[:, sl],
                                                                           0.0, ALU.add, ALU.bypass),
                           reads=[r_ahm], writes=[r_rcs], tag="r")
                        em("vector", lambda e, sl=sl: e.tensor_sub(w_hm[:], b_hm[:, sl], r_cs[:]),
                           reads=[r_bhm, r_rcs], writes=[r_whm], tag="w")
                        em("vector", lambda e: e.tensor_tensor_scan(M_hm[:], w_hm[:], w_hm[:], -1e30, ALU.max, ALU.max),
                           reads=[r_whm], writes=[r_Mhm], tag="M")
                        em("vector", lambda e: e.tensor_scalar(negM[:], M_hm[:], -1.0, None, ALU.mult),
                           reads=[r_Mhm], writes=[r_negM], tag="negm")
                        em("scalar", lambda e: e.dma_start(
                            out=negM_cat[0:1, :].rearrange("p (hh c) -> p hh c", hh=8), in_=negM[:]),
                            reads=[r_negM], writes=[r_cat], sem=s_cat, inc=16, tag="cat")
                        for hf in range(2):
                            hs = slice(hf * 512, (hf + 1) * 512)
                            mmg([
                                (lambda e, hs=hs: e.matmul(pA[:, hs], w_hm[:], s8[:, hs], start=True, stop=False)),
                                (lambda e, hs=hs: e.matmul(pA[:, hs], ones[0:1, :], negM_cat[0:1, hs], start=False, stop=False)),
                                (lambda e, hs=hs: e.matmul(pA[:, hs], tribig[:], itile[:, hs], start=False, stop=True)),
                            ], reads=[r_whm, r_cat, r_const], writes=[rA[hf]], tag="Dmm")
                        em("scalar", lambda e: e.activation(Emm[:], pA[:], AF.Exp, bias=0.0, scale=1.0),
                           reads=[rA[0], rA[1]], writes=[r_Emm], tag="E")
                        for hf in range(2):
                            hs = slice(hf * 512, (hf + 1) * 512)
                            mmg([
                                (lambda e, hs=hs: e.matmul(pB[0:1, hs], lam[:], s8[:, hs], start=True, stop=False)),
                                (lambda e, hs=hs: e.matmul(pB[0:1, hs], ones[0:1, 0:1], negM_cat[0:1, hs], start=False, stop=True)),
                            ], reads=[r_lam, r_cat, r_const], writes=[rB[hf]], tag="zmm")
                        em("scalar", lambda e: e.activation(zc_row[:], pB[0:1, :], AF.Exp, bias=0.0, scale=1.0),
                           reads=[rB[0], rB[1]], writes=[r_zc], tag="zc")
                        for hf in range(2):
                            hs = slice(hf * 512, (hf + 1) * 512)
                            mmg([(lambda e, hs=hs: e.matmul(pB[0:1, hs], onesb[:], Emm[:, hs], start=True, stop=True))],
                                reads=[r_Emm, r_const, r_zc], writes=[rB[hf]], tag="cs")
                        em("vector", lambda e: e.tensor_add(zrow[:], zc_row[:], pB[0:1, :]),
                           reads=[r_zc, rB[0], rB[1]], writes=[r_z], tag="z")
                        em("scalar", lambda e: e.activation(zrow[:], zrow[:], AF.Ln, bias=0.0, scale=1.0),
                           reads=[r_z], writes=[r_z], tag="izln")
                        em("scalar", lambda e: e.activation(izrow, zrow[:], AF.Exp, bias=0.0, scale=cst[0:1, 2:3]),
                           reads=[r_z, r_const], writes=[r_iz], tag="izr")
                        em("vector", lambda e: e.tensor_mul(c_row, zc_row[:], izrow),
                           reads=[r_zc, r_iz], writes=[r_cr], tag="cr")
                        em("scalar", lambda e: e.dma_start(
                            out=cizhm[:, 0:128], in_=ciz[0:1, 0:1024].rearrange("p (hh c) -> p hh c", hh=8)),
                            reads=[r_cr], writes=[r_chm], sem=s_c, inc=16, tag="chm")
                        em("scalar", lambda e: e.dma_start(
                            out=cizhm[:, 128:256], in_=ciz[0:1, 1024:2048].rearrange("p (hh c) -> p hh c", hh=8)),
                            reads=[r_iz], writes=[r_izhm], sem=s_iz, inc=16, tag="izhm")
                        em("vector", lambda e: e.tensor_scalar(c_hmb[:], c_hm, 1.0, None, ALU.mult),
                           reads=[r_chm], writes=[r_chmb], tag="chb")
                        for hf in range(2):
                            hs = slice(hf * 512, (hf + 1) * 512)
                            mmg([(lambda e, hs=hs: e.matmul(pB[:, hs], ones[0:1, :], izrow[0:1, hs], start=True, stop=True))],
                                reads=[r_iz, r_const], writes=[rB[hf]], tag="ib")
                        em("vector", lambda e: e.tensor_mul(Ett[:], Emm[:], pB[:]),
                           reads=[r_Emm, rB[0], rB[1]], writes=[r_qs], tag="Et")
                        em("vector", lambda e: e.tensor_add(wcol[:], Ett[:, 0:512], Ett[:, 512:1024]),
                           reads=[r_qs], writes=[r_wcol], tag="c1")
                        em("vector", lambda e: e.tensor_add(wcol[:, 0:256], wcol[:, 0:256], wcol[:, 256:512]),
                           reads=[r_wcol], writes=[r_wcol], tag="c2")
                        em("vector", lambda e: e.tensor_add(wtot[:], wcol[:, 0:128], wcol[:, 128:256]),
                           reads=[r_wcol], writes=[r_wtot], tag="wtot")
                        for hf in range(2):
                            hs = slice(hf * 512, (hf + 1) * 512)
                            vsl = slice(jj * 1024 + hf * 512, jj * 1024 + (hf + 1) * 512)
                            mmg([
                                (lambda e, hs=hs, vsl=vsl: e.matmul(pA[:, hs], wtot[:], vt_t[:, vsl], start=True, stop=False)),
                                (lambda e, hs=hs: e.matmul(pA[:, hs], c_hmb[:], Rb[:, hs], start=False, stop=True)),
                            ], reads=[r_wtot, r_vt, r_chmb, r_Rb], writes=[rA[hf]], tag="Ymm")
                        Esl = Emm[:].rearrange("p (hh c) -> p hh c", hh=8)[:, :, 127:128].squeeze(2)
                        for hf in range(2):
                            hs = slice(hf * 512, (hf + 1) * 512)
                            vsl = slice(jj * 1024 + hf * 512, jj * 1024 + (hf + 1) * 512)
                            mmg([(lambda e, hs=hs, vsl=vsl: e.matmul(pN[:, hs], Esl, vt_t[:, vsl], start=True, stop=True))],
                                reads=[r_Emm, r_vt], writes=[rN[hf]], tag="NE")
                        em("scalar", lambda e: e.activation(y_tm[:], pA[:], AF.Copy, bias=0.0, scale=1.0),
                           reads=[rA[0], rA[1]], writes=[r_ytm], tag="yev")
                        for g2 in range(2):
                            srcs = [y_tm[:, (g2 * 4 + q2) * 128:(g2 * 4 + q2 + 1) * 128] for q2 in range(4)]
                            dv = y_fm[:].rearrange("p (kc t) -> p kc t", t=512)[:, g2 * 4:(g2 + 1) * 4,
                                                                               jj * 128:(jj + 1) * 128]
                            transpose_tiles(srcs, pC[:, g2 * 512:(g2 + 1) * 512], rC[g2], dv,
                                            r_ytm, r_yfm, tag="yT")
                        em("vector", lambda e: e.tensor_scalar(Rt[:], Rt[:], cizhm[:, 127:128], None, ALU.mult),
                           reads=[r_R, r_chm], writes=[r_R], tag="Rdec")
                        em("vector", lambda e: e.scalar_tensor_tensor(Rt[:], pN[:], cizhm[:, 255:256], Rt[:],
                                                                      ALU.mult, ALU.add),
                           reads=[rN[0], rN[1], r_izhm, r_R], writes=[r_R], tag="R")
                        em("vector", lambda e: e.tensor_scalar(Rb[:], Rt[:], 1.0, None, ALU.mult),
                           reads=[r_R], writes=[r_Rb], tag="Rbc")
                        em("scalar", lambda e: e.activation(lnzt[:], cizhm[:, 255:256], AF.Ln, bias=0.0, scale=1.0),
                           reads=[r_izhm], writes=[r_lnz], tag="lnu")
                        em("vector", lambda e: e.tensor_sub(lam[:], M_hm[:, 127:128], lnzt[:]),
                           reads=[r_Mhm, r_lnz], writes=[r_lam], tag="lam1")
                        em("vector", lambda e: e.tensor_add(lam[:], lam[:], r_cs[:, 127:128]),
                           reads=[r_rcs, r_lam], writes=[r_lam], tag="lam2")
                    for sub in range(NSUB if "f" in sections else 0):
                        ri = sub % 2
                        em("sync", lambda e, b=b, t0=t0, sub=sub, ri=ri: e.dma_start(
                            out=xrt[ri][:], in_=x_in[b, t0 + sub * 128:t0 + (sub + 1) * 128, :]),
                            writes=[r_xrt[ri]], extra=bar0, sem=s_xr[ri], inc=16, tag="xrld")
                        for hf in range(2):
                            psC = pC[:, hf * 512:(hf + 1) * 512]
                            mmg([(lambda e, kc=kc, sub=sub, hf=hf, psC=psC: e.matmul(
                                psC, y_fm[:, kc * 512 + sub * 128: kc * 512 + (sub + 1) * 128],
                                wot[:, kc * 1024 + hf * 512: kc * 1024 + (hf + 1) * 512],
                                start=(kc == 0), stop=(kc == 7))) for kc in range(8)],
                                reads=[r_yfm, r_wot], writes=[rC[hf]], tag="omm")
                            em("vector", lambda e, hf=hf, psC=psC, ri=ri: e.tensor_add(
                                x2sb[:, hf * 512:(hf + 1) * 512], psC, xrt[ri][:, hf * 512:(hf + 1) * 512]),
                                reads=[rC[hf], r_xrt[ri]], writes=[r_x2] if hf == 1 else [], tag="x2")
                        em("sync", lambda e, b=b, t0=t0, sub=sub: e.dma_start(
                            out=x2_d[b * S + t0 + sub * 128: b * S + t0 + (sub + 1) * 128, :], in_=x2sb[:]),
                            reads=[r_x2], writes=[r_x2d], sem=s_sx2, inc=16, tag="x2st")
                        ln_emit(x2sb[:], r_x2, h2sb[:], r_h2, "ln2")
                        quant(h2sb[:], 1024, h2q2[sub % 2][:], r_h2, r_h2q2[sub % 2], qsA, tag="h2q")
                        if pendA is not None:
                            h2_tail(*pendA)
                        pendA = (b, t0, sub)

            if pendA is not None:
                h2_tail(*pendA)
        barA = engine_barrier("barA")
        barA += [r_x2d.writer, r_h2d.writer]
        ph0A.close()
        if stage < 2:
            n_tail_sem, n_tail = (s_scr, P.scnt.get(id(s_scr), 0)) if stage == 0 else (s_sh2, P.scnt.get(id(s_sh2), 0))
            block = eg(nc.Block())
            P.finalize(block, tail={"sync": lambda eng: eng.wait_ge(n_tail_sem, n_tail)})
            return nc
        with ExitStack() as phB:
            eB = phB.enter_context
            wqg = eB(nc.sbuf_tensor("wqg", [128, 8 * I], BF16))
            wqu = eB(nc.sbuf_tensor("wqu", [128, 8 * I], BF16))
            wqd = eB(nc.sbuf_tensor("wqd", [128, 22 * 1024], BF16))
            x2t = [eB(nc.sbuf_tensor(f"x2t{i}", [128, 1024], F32)) for i in range(2)]
            h2f = [eB(nc.sbuf_tensor(f"h2f{i}", [128, 8 * 128], BF16)) for i in range(2)]
            hm_tm = eB(nc.sbuf_tensor("hm_tm", [128, I], F32))
            hmq2 = [eB(nc.sbuf_tensor(f"hmq{i}", [128, I], BF16)) for i in range(2)]
            qsB = alloc_qscr(eB, I, "pB")
            hmqf = qsB["q_u"]
            r_wqg, r_wqu, r_wqd = Res("wqg"), Res("wqu"), Res("wqd")
            r_x2t = [Res("x2t0"), Res("x2t1")]
            r_h2f = [Res("h2f0"), Res("h2f1")]
            r_hm = Res("hm")
            r_hmq2 = [Res("hmq0"), Res("hmq1")]
            r_hmqf = r_qs

            em("sync", lambda e: e.dma_start(
                out=wqg[:].rearrange("p (kc r) -> p kc r", r=I),
                in_=wqgt_d.rearrange("(kc p) r -> p kc r", p=128)),
                reads=[r_wqd_scr], writes=[r_wqg], extra=barA, sem=s_w, inc=16, tag="wgld")
            em("sync", lambda e: e.dma_start(
                out=wqu[:].rearrange("p (kc r) -> p kc r", r=I),
                in_=wqut_d.rearrange("(kc p) r -> p kc r", p=128)),
                reads=[r_wqd_scr], writes=[r_wqu], extra=barA, sem=s_w, inc=16, tag="wuld")
            t_wd = em("sync", lambda e: e.dma_start(
                out=wqd[:].rearrange("p (kc r) -> p kc r", r=1024),
                in_=wqdt_d.rearrange("(kc p) r -> p kc r", p=128)),
                reads=[r_wqd_scr], writes=[r_wqd], extra=barA, sem=s_w, inc=16, tag="wdld")
            r_wqg.writer = t_wd
            r_wqu.writer = t_wd

            NSl = [(i * 512, min(I, (i + 1) * 512)) for i in range((I + 511) // 512)]

            def sub_head(b, t0, sub, rx):
                hmq = hmq2[sub % 2]
                for gq in range(0, 22, 4):
                    ng = min(4, 22 - gq)
                    srcs = [hmq[:, (gq + q2) * 128:(gq + q2 + 1) * 128] for q2 in range(ng)]
                    transpose_tiles(srcs, pC[:, 0:512], rC[0],
                                    hmqf[:, gq * 128:(gq + ng) * 128],
                                    r_hmq2[sub % 2], r_hmqf, tag="hmT")

            def sub_tail(b, t0, sub, rx):
                outsb = x2t[rx]
                for hf in range(2):
                    mmg([(lambda e, kc=kc, hf=hf: e.matmul(
                        pC[:, 512:1024], hmqf[:, kc * 128:(kc + 1) * 128],
                        wqd[:, kc * 1024 + hf * 512: kc * 1024 + (hf + 1) * 512],
                        start=(kc == 0), stop=(kc == 21))) for kc in range(22)],
                        reads=[r_hmqf, r_wqd], writes=[rC[1]], tag="dmm")
                    em("vector", lambda e, hf=hf, rx=rx: e.tensor_add(
                        outsb[:, hf * 512:(hf + 1) * 512], pC[:, 512:1024],
                        x2t[rx][:, hf * 512:(hf + 1) * 512]),
                        reads=[rC[1]], writes=[r_x2t[rx]] if hf == 1 else [], tag="oadd")
                em("sync", lambda e, b=b, t0=t0, sub=sub, rx=rx: e.dma_start(
                    out=out_d[b, t0 + sub * 128: t0 + (sub + 1) * 128, :], in_=x2t[rx][:]),
                    reads=[r_x2t[rx]], sem=s_out, inc=16, tag="outdma")

            pend = None
            for b in range(Bc):
                for T in range(NT):
                    t0 = T * TT
                    ti = b * NT + T
                    for sub in range(NSUB):
                        rx = sub % 2
                        ri = sub % 2
                        em("sync", lambda e, b=b, t0=t0, sub=sub, ri=ri: e.dma_start(
                            out=h2f[ri][:].rearrange("p (kc t) -> p kc t", t=128),
                            in_=h2qt_d.rearrange("(kc p) t -> p kc t", p=128)[:, :, b * S + t0 + sub * 128: b * S + t0 + (sub + 1) * 128]),
                            reads=[r_h2d], writes=[r_h2f[ri]], extra=barA, sem=s_fl[ri], inc=16, tag="h2ld")
                        em("sync", lambda e, b=b, t0=t0, sub=sub, rx=rx: e.dma_start(
                            out=x2t[rx][:], in_=x2_d[b * S + t0 + sub * 128: b * S + t0 + (sub + 1) * 128, :]),
                            reads=[r_x2d], writes=[r_x2t[rx]], extra=barA, sem=s_x[rx], inc=16, tag="x2ld")
                        for si, (n0, n1) in enumerate(NSl):
                            nn = n1 - n0
                            pg = pA[:, (si % 2) * 512:(si % 2) * 512 + nn]
                            pu = pB[:, (si % 2) * 512:(si % 2) * 512 + nn]
                            mmg([(lambda e, kc=kc, n0=n0, n1=n1, pg=pg, ri=ri: e.matmul(
                                pg, h2f[ri][:, kc * 128:(kc + 1) * 128],
                                wqg[:, kc * I + n0: kc * I + n1], start=(kc == 0), stop=(kc == 7)))
                                for kc in range(8)], reads=[r_h2f[ri], r_wqg], writes=[rA[si % 2]], tag="gmm")
                            em("scalar", lambda e, pg=pg, n0=n0, n1=n1: e.activation(
                                hm_tm[:, n0:n1], pg, AF.Silu, bias=0.0, scale=1.0),
                               reads=[rA[si % 2]], writes=[r_hm], tag="silu")
                            mmg([(lambda e, kc=kc, n0=n0, n1=n1, pu=pu, ri=ri: e.matmul(
                                pu, h2f[ri][:, kc * 128:(kc + 1) * 128],
                                wqu[:, kc * I + n0: kc * I + n1], start=(kc == 0), stop=(kc == 7)))
                                for kc in range(8)], reads=[r_h2f[ri], r_wqu], writes=[rB[si % 2]], tag="umm")
                            em("vector", lambda e, pu=pu, n0=n0, n1=n1: e.tensor_tensor(
                                hm_tm[:, n0:n1], hm_tm[:, n0:n1], pu, ALU.mult),
                                reads=[rB[si % 2], r_hm], writes=[r_hm], tag="hmid")
                        quant(hm_tm[:], I, hmq2[sub % 2][:], r_hm, r_hmq2[sub % 2], qsB, tag="hmq")
                        if pend is not None:
                            sub_head(*pend)
                            sub_tail(*pend)
                        pend = (b, t0, sub, rx)
            sub_head(*pend)
            sub_tail(*pend)

            n_out_total = P.scnt[id(s_out)]

        block = eg(nc.Block())
        P.finalize(block, tail={"sync": lambda eng: eng.wait_ge(s_out, n_out_total)})
    return nc


def kernel(L0_attn_x, L0_attn_W_q, L0_attn_W_k, L0_attn_W_v, L0_attn_W_alpha_w,
           L0_attn_W_alpha_b, L0_attn_W_o, L0_ffn_w_gate, L0_ffn_w_up, L0_ffn_w_down):
    n = 8
    Bc = 2
    x = np.ascontiguousarray(L0_attn_x, dtype=np.float32)
    hc = host_consts()
    walb = np.ascontiguousarray(np.asarray(L0_attn_W_alpha_b, np.float32).reshape(4, 128).T)
    common = dict(
        wq_in=np.ascontiguousarray(L0_attn_W_q, np.float32),
        wk_in=np.ascontiguousarray(L0_attn_W_k, np.float32),
        wal_in=np.ascontiguousarray(L0_attn_W_alpha_w, np.float32),
        wv_in=np.ascontiguousarray(L0_attn_W_v, np.float32),
        wo_in=np.ascontiguousarray(L0_attn_W_o, np.float32),
        walb_in=walb,
        wg_in=np.ascontiguousarray(L0_ffn_w_gate, np.float32),
        wu_in=np.ascontiguousarray(L0_ffn_w_up, np.float32),
        wd_in=np.ascontiguousarray(L0_ffn_w_down, np.float32),
        **{k: np.ascontiguousarray(v) for k, v in hc.items()},
    )
    in_maps = []
    for c in range(n):
        m = dict(common)
        m["x_in"] = np.ascontiguousarray(x[c * Bc:(c + 1) * Bc])
        in_maps.append(m)
    nc = build_core(Bc=Bc, S=2048)
    res = run_bass_kernel_spmd(nc, in_maps, core_ids=list(range(n)))
    out = np.concatenate([np.asarray(r["out"], np.float32) for r in res.results], axis=0)
    return out


# version 11
# speedup vs baseline: 1.0629x; 1.0629x over previous
import numpy as np
import ml_dtypes
import concourse.bass as bass
import concourse.mybir as mybir
from concourse.bass_utils import run_bass_kernel_spmd
from contextlib import ExitStack

F32 = mybir.dt.float32
BF16 = mybir.dt.bfloat16
AF = mybir.ActivationFunctionType
ALU = mybir.AluOpType
AX = mybir.AxisListType

D = 1024
H = 8
I = 2816
C = 128
TT = 512
EPS = 1e-8
BF = ml_dtypes.bfloat16


class Tok:
    __slots__ = ("eng", "seq", "sem", "val")
    def __init__(self, eng, seq, sem, val):
        self.eng, self.seq, self.sem, self.val = eng, seq, sem, val


class Res:
    __slots__ = ("writer", "readers", "name")
    def __init__(self, name=""):
        self.writer = None
        self.readers = []
        self.name = name


class Prog:
    ENGS = ("tensor", "vector", "scalar", "gpsimd", "sync")

    def __init__(self, nc, sems):
        self.nc = nc
        self.ops = {e: [] for e in self.ENGS}
        self.scnt = {}
        self.watermark = {e: {} for e in self.ENGS}
        self.engsem = sems

    def raw_emit(self, eng, fn, deps, sem, inc, tag):
        waits = {}
        for d in deps:
            if d is None or d.eng == eng:
                continue
            sid = id(d.sem)
            if waits.get(sid, (None, 0))[1] < d.val:
                waits[sid] = (d.sem, d.val)
        todo = []
        wm = self.watermark[eng]
        for sid, (s, v) in waits.items():
            if wm.get(sid, 0) < v:
                todo.append((s, v))
                wm[sid] = v
        if inc:
            self.scnt[id(sem)] = self.scnt.get(id(sem), 0) + inc
            tok = Tok(eng, len(self.ops[eng]), sem, self.scnt[id(sem)])
        else:
            tok = Tok(eng, len(self.ops[eng]), None, None)
        self.ops[eng].append((todo, fn, sem if inc else None, inc, tag))
        return tok

    def finalize(self, block, tail=None):
        def make_body(eng_name):
            def body(eng):
                for todo, fn, sem, inc, tag in self.ops[eng_name]:
                    for (s, v) in todo:
                        eng.wait_ge(s, v)
                    ins = fn(eng)
                    if inc:
                        ins.then_inc(sem, inc)
                if tail and eng_name in tail:
                    tail[eng_name](eng)
            return body
        for e in ("sync", "vector", "scalar", "tensor", "gpsimd"):
            getattr(block, e)(make_body(e))


def host_consts():
    s8 = np.zeros((8, 1024), np.float32)
    for h in range(8):
        s8[h, h * 128:(h + 1) * 128] = 1.0
    tribig = np.where(np.arange(128)[None, :] > np.arange(128)[:, None], -1e30, 0.0).astype(np.float32)
    itile = np.tile(np.eye(128, dtype=np.float32), (1, 8))
    ones = np.ones((128, 128), np.float32)
    onesb = np.ones((128, 1), BF)
    i128b = np.eye(128, dtype=np.float32).astype(BF)
    maskb = np.zeros((128, 32), np.float32)
    maska = np.zeros((128, 32), np.float32)
    for kc in range(4):
        for r in range(128):
            h = (kc * 128 + r) // 64
            maskb[r, kc * 8 + h] = 0.125
            maska[r, kc * 8 + h] = -1.0
    cvals = np.array([-0.5, 1e-5, -1.0, 1.0 / 1024.0, 0.0], np.float32)
    cst = np.tile(cvals[None, :], (128, 1)).astype(np.float32)
    return dict(c_s8=s8, c_tribig=tribig, c_itile=itile, c_ones=ones,
                c_onesb=onesb, c_i128b=i128b, c_maskb=maskb.astype(BF),
                c_maska=maska.astype(BF), c_cst=cst)


def build_core(Bc=2, S=2048, stage=2, sections="abcdefgh12345"):
    NCH = S // C
    NT = S // TT
    CPT = TT // C
    NSUB = TT // 128

    nc = bass.Bass("TRN2", target_bir_lowering=False, detect_race_conditions=False)
    x_in = nc.dram_tensor("x_in", [Bc, S, D], F32, kind="ExternalInput").ap()
    wq_in = nc.dram_tensor("wq_in", [512, 1024], F32, kind="ExternalInput").ap()
    wk_in = nc.dram_tensor("wk_in", [512, 1024], F32, kind="ExternalInput").ap()
    wal_in = nc.dram_tensor("wal_in", [512, 1024], F32, kind="ExternalInput").ap()
    wv_in = nc.dram_tensor("wv_in", [1024, 1024], F32, kind="ExternalInput").ap()
    wo_in = nc.dram_tensor("wo_in", [1024, 1024], F32, kind="ExternalInput").ap()
    walb_in = nc.dram_tensor("walb_in", [128, 4], F32, kind="ExternalInput").ap()
    wg_in = nc.dram_tensor("wg_in", [I, 1024], F32, kind="ExternalInput").ap()
    wu_in = nc.dram_tensor("wu_in", [I, 1024], F32, kind="ExternalInput").ap()
    wd_in = nc.dram_tensor("wd_in", [1024, I], F32, kind="ExternalInput").ap()
    hc = host_consts()
    cin = {}
    for k, v in hc.items():
        dt = BF16 if v.dtype == BF else F32
        cin[k] = nc.dram_tensor(k, list(v.shape), dt, kind="ExternalInput").ap()
    out_d = nc.dram_tensor("out", [Bc, S, D], F32, kind="ExternalOutput").ap()
    x2_d = nc.dram_tensor("x2_d", [Bc * S, D], F32, kind="Internal").ap()
    h2qt_d = nc.dram_tensor("h2qt_d", [D, Bc * S], BF16, kind="Internal").ap()
    wqgt_d = nc.dram_tensor("wqgt_d", [1024, I], BF16, kind="Internal").ap()
    wqut_d = nc.dram_tensor("wqut_d", [1024, I], BF16, kind="Internal").ap()
    wqdt_d = nc.dram_tensor("wqdt_d", [I, 1024], BF16, kind="Internal").ap()

    with ExitStack() as g:
        eg = g.enter_context
        s_pe = eg(nc.semaphore("s_pe"))
        s_dve = eg(nc.semaphore("s_dve"))
        s_act = eg(nc.semaphore("s_act"))
        s_gp = eg(nc.semaphore("s_gp"))
        s_const = eg(nc.semaphore("s_const"))
        s_w = eg(nc.semaphore("s_w"))
        s_x = [eg(nc.semaphore(f"s_x{i}")) for i in range(2)]
        s_xr = [eg(nc.semaphore(f"s_xr{i}")) for i in range(2)]
        s_cat = eg(nc.semaphore("s_cat"))
        s_c = eg(nc.semaphore("s_c"))
        s_iz = eg(nc.semaphore("s_iz"))
        s_scr = eg(nc.semaphore("s_scr"))
        s_sx2 = eg(nc.semaphore("s_sx2"))
        s_sh2 = eg(nc.semaphore("s_sh2"))
        s_fl = [eg(nc.semaphore(f"s_fl{i}")) for i in range(2)]
        s_out = eg(nc.semaphore("s_out"))

        P = Prog(nc, {"tensor": s_pe, "vector": s_dve, "scalar": s_act,
                      "gpsimd": s_gp, "sync": s_const})

        def em(eng, fn, reads=(), writes=(), extra=(), sem=None, inc=1, tag=""):
            deps = list(extra)
            for r in reads:
                deps.append(r.writer)
            for w in writes:
                deps.append(w.writer)
                deps.extend(w.readers)
            tok = P.raw_emit(eng, fn, deps, sem or P.engsem[eng], inc, tag)
            for r in reads:
                r.readers.append(tok)
            for w in writes:
                w.writer = tok
                w.readers = []
            return tok

        def mmg(parts, reads=(), writes=(), extra=(), tag="mm"):
            deps = list(extra)
            for r in reads:
                deps.append(r.writer)
            for w in writes:
                deps.append(w.writer)
                deps.extend(w.readers)
            tok = None
            n = len(parts)
            for i, fn in enumerate(parts):
                tok = P.raw_emit("tensor", fn, deps if i == 0 else (), s_pe,
                                 1 if i == n - 1 else 0, tag)
            for r in reads:
                r.readers.append(tok)
            for w in writes:
                w.writer = tok
                w.readers = []
            return tok

        pA = eg(nc.psum_tensor("pA", [128, 1024], F32))
        pB = eg(nc.psum_tensor("pB", [128, 1024], F32))
        pC = eg(nc.psum_tensor("pC", [128, 1024], F32))
        pN = eg(nc.psum_tensor("pN", [8, 1024], F32))
        rA = [Res("pA0"), Res("pA1")]
        rB = [Res("pB0"), Res("pB1")]
        rC = [Res("pC0"), Res("pC1")]
        rN = [Res("pN0"), Res("pN1")]

        i128b = eg(nc.sbuf_tensor("i128b", [128, 128], BF16))
        cst = eg(nc.sbuf_tensor("cst", [128, 6], F32))
        dummy = eg(nc.sbuf_tensor("barbuf", [1, 8], F32))
        walb = eg(nc.sbuf_tensor("walb", [128, 4], F32))
        r_dummy = Res("dummy")

        def engine_barrier(tagb):
            tv = em("vector", lambda e: e.memset(dummy[:, 0:4], 0.0), writes=[r_dummy], tag=tagb)
            ta = em("scalar", lambda e: e.activation(dummy[:, 4:8], dummy[:, 0:4], AF.Identity, bias=0.0, scale=1.0),
                    reads=[r_dummy], tag=tagb)
            tp = mmg([(lambda e: e.matmul(pN[0:1, 0:16], ones[0:1, 0:1], ones[0:1, 0:16], start=True, stop=True))],
                     reads=[r_const], writes=[rN[0]], tag=tagb)
            return [tv, ta, tp]
        r_qs = Res("qscratch")

        def alloc_qscr(ee, KW, pfx, alias_k=True):
            d_u = ee(nc.sbuf_tensor(pfx + "q_u", [128, KW], BF16))
            return dict(
                q_absf=ee(nc.sbuf_tensor(pfx + "q_absf", [128, KW], F32)),
                q_u=d_u,
                q_h=ee(nc.sbuf_tensor(pfx + "q_h", [128, KW], BF16)),
                q_k=d_u if alias_k else ee(nc.sbuf_tensor(pfx + "q_k", [128, KW], BF16)),
                q_sgn=ee(nc.sbuf_tensor(pfx + "q_sgn", [128, KW], BF16)),
                q_scl=ee(nc.sbuf_tensor(pfx + "q_scl", [128, KW // 16], F32)),
                q_inv=ee(nc.sbuf_tensor(pfx + "q_inv", [128, KW // 16], F32)),
            )

        r_wqt, r_wkt, r_walt, r_wvt, r_wot = (Res(x) for x in ("wqt", "wkt", "walt", "wvt", "wot"))
        r_const = Res("const")
        r_x2d = Res("x2_d")
        r_h2d = Res("h2qt_d")
        r_wqd_scr = Res("wq_scr")

        c_neghalf = cst[:, 0:1]
        c_lneps = cst[:, 1:2]
        c_negone = cst[:, 2:3]

        def quant(x_bf, K, out_bf, r_in, r_out, qs, rqs=None, kf32=True, tag="q"):
            if rqs is None:
                rqs = r_qs
            nb = K // 16
            q_absf, q_u, q_h, q_k, q_sgn, q_scl, q_inv = (
                qs["q_absf"], qs["q_u"], qs["q_h"], qs["q_k"], qs["q_sgn"],
                qs["q_scl"], qs["q_inv"])
            em("scalar", lambda e: e.activation(q_absf[:, 0:K], x_bf, AF.Abs, bias=0.0, scale=1.0),
               reads=[r_in], writes=[rqs], tag=tag + "abs")
            em("scalar", lambda e: e.activation(q_sgn[:, 0:K], x_bf, AF.Sign, bias=0.0, scale=1.0),
               reads=[r_in, rqs], tag=tag + "sgn")
            av = q_absf[:, 0:K].rearrange("p (n k) -> p n k", k=16)
            em("vector", lambda e: e.tensor_reduce(q_scl[:, 0:nb].unsqueeze(2), av, AX.X, ALU.max,
                                                   apply_absolute_value=True), reads=[rqs], tag=tag + "red")
            em("vector", lambda e: e.tensor_scalar(q_scl[:, 0:nb], q_scl[:, 0:nb], 1.0 / 6.0, 1e-12, ALU.mult, ALU.max),
               reads=[rqs], tag=tag)
            em("vector", lambda e: e.reciprocal(q_inv[:, 0:nb], q_scl[:, 0:nb]), reads=[rqs], tag=tag)
            iv = q_inv[:, 0:nb].unsqueeze(2).broadcast_to([128, nb, 16])
            em("vector", lambda e: e.tensor_tensor(av, av, iv, ALU.mult), reads=[rqs], tag=tag + "u")
            uf = q_absf[:, 0:K]
            em("vector", lambda e: e.tensor_scalar(q_u[:, 0:K], uf, 1.0, None, ALU.mult), reads=[rqs], tag=tag + "ub")
            u = q_u[:, 0:K]
            em("vector", lambda e: e.tensor_scalar(q_h[:, 0:K], u, 0.25, None, ALU.is_gt), reads=[rqs], tag=tag)
            for th in (0.75, 1.25, 1.75):
                em("vector", lambda e, th=th: e.scalar_tensor_tensor(q_h[:, 0:K], u, th, q_h[:, 0:K], ALU.is_gt, ALU.add),
                   reads=[rqs], tag=tag)
            uk = uf if kf32 else u
            em("vector", lambda e, uk=uk: e.tensor_scalar(q_k[:, 0:K], uk, 5.0, None, ALU.is_gt), reads=[rqs], tag=tag)
            for th in (5.0, 3.5, 2.5):
                em("vector", lambda e, th=th, uk=uk: e.scalar_tensor_tensor(q_k[:, 0:K], uk, th, q_k[:, 0:K], ALU.is_gt, ALU.add),
                   reads=[rqs], tag=tag)
            em("vector", lambda e: e.scalar_tensor_tensor(q_h[:, 0:K], q_h[:, 0:K], 0.5, q_k[:, 0:K], ALU.mult, ALU.add),
               reads=[rqs], tag=tag)
            em("vector", lambda e: e.tensor_mul(q_h[:, 0:K], q_h[:, 0:K], q_sgn[:, 0:K]), reads=[rqs], tag=tag)
            sb = q_scl[:, 0:nb].unsqueeze(2).broadcast_to([128, nb, 16])
            t_out = em("vector", lambda e: e.tensor_tensor(
                out_bf.rearrange("p (n k) -> p n k", k=16),
                q_h[:, 0:K].rearrange("p (n k) -> p n k", k=16), sb, ALU.mult),
                reads=[rqs], writes=[r_out], tag=tag + "out")
            return t_out

        def transpose_tiles(srcs, psum_ap, psum_res, dst_view, r_src, r_dst, extra=(), tag="tr"):
            n = len(srcs)
            parts = [(lambda e, i=i, s=srcs[i]: e.matmul(psum_ap[:, i * 128:(i + 1) * 128], s, i128b[:],
                                                          start=True, stop=True)) for i in range(n)]
            mm_t = mmg(parts, reads=[r_src, r_const], writes=[psum_res], extra=extra, tag=tag)
            return em("scalar", lambda e: e.activation(
                dst_view, psum_ap[:, 0:n * 128], AF.Copy, bias=0.0, scale=1.0),
                reads=[psum_res], writes=[r_dst], tag=tag + "ev")

        ph0A = ExitStack()
        e0A = ph0A.enter_context
        s8 = e0A(nc.sbuf_tensor("s8", [8, 1024], F32))
        tribig = e0A(nc.sbuf_tensor("tribig", [128, 128], F32))
        itile = e0A(nc.sbuf_tensor("itile", [128, 1024], F32))
        ones = e0A(nc.sbuf_tensor("ones", [1, 128], F32))
        onesb = e0A(nc.sbuf_tensor("onesb", [128, 1], BF16))
        maskb = e0A(nc.sbuf_tensor("maskb", [128, 32], BF16))
        maska = e0A(nc.sbuf_tensor("maska", [128, 32], BF16))
        wqt = e0A(nc.sbuf_tensor("wqt", [128, 8 * 512], BF16))
        wkt = e0A(nc.sbuf_tensor("wkt", [128, 8 * 512], BF16))
        walt = e0A(nc.sbuf_tensor("walt", [128, 8 * 512], BF16))
        wvt = e0A(nc.sbuf_tensor("wvt", [128, 8 * 1024], BF16))
        wot = e0A(nc.sbuf_tensor("wot", [128, 8 * 1024], BF16))
        CN = {"c_s8": s8, "c_tribig": tribig, "c_itile": itile, "c_ones": ones,
              "c_onesb": onesb, "c_i128b": i128b, "c_maskb": maskb, "c_maska": maska}
        for k in CN:
            em("sync", lambda e, t=CN[k], s=cin[k]: e.dma_start(out=t[:], in_=s if t is not ones else s[0:1, :]),
               sem=s_const, inc=16, tag="ldc")
        em("sync", lambda e: e.dma_start(out=cst[:, 0:5], in_=cin["c_cst"]), sem=s_const, inc=16, tag="ldc")
        t_ld = em("sync", lambda e: e.dma_start(out=walb[:], in_=walb_in),
                  writes=[r_const], sem=s_const, inc=16, tag="ldc")
        with ExitStack() as ph0:
            e0 = ph0.enter_context
            wld = e0(nc.sbuf_tensor("wld", [128, 8192], F32))
            wldb = e0(nc.sbuf_tensor("wldb", [128, 8192], BF16))
            r_wld, r_wldb = Res("wld"), Res("wldb")

            def prep_attn_w(src, rows, dstT, r_dst, ncols, tag):
                nrt = rows // 128
                em("sync", lambda e: e.dma_start(
                    out=wld[:, 0:nrt * 1024].rearrange("p (rt k) -> p rt k", k=1024),
                    in_=src.rearrange("(rt p) k -> p rt k", p=128)),
                   writes=[r_wld], sem=s_w, inc=16, tag=tag + "ld")
                em("scalar", lambda e: e.activation(wldb[:, 0:nrt * 1024], wld[:, 0:nrt * 1024],
                                                    AF.Copy, bias=0.0, scale=1.0),
                   reads=[r_wld], writes=[r_wldb], extra=[t_ld], tag=tag + "cast")
                for rt in range(nrt):
                    for g2 in range(2):
                        srcs = [wldb[:, rt * 1024 + (g2 * 4 + q2) * 128: rt * 1024 + (g2 * 4 + q2 + 1) * 128]
                                for q2 in range(4)]
                        dv = dstT[:].rearrange("p (kc n) -> p kc n", n=ncols)[:, g2 * 4:(g2 + 1) * 4,
                                                                             rt * 128:(rt + 1) * 128]
                        pi = (rt * 2 + g2) % 2
                        transpose_tiles(srcs, pC[:, pi * 512:(pi + 1) * 512], rC[pi], dv,
                                        r_wldb, r_dst, tag=tag + "tr")

            prep_attn_w(wq_in, 512, wqt, r_wqt, 512, "wq")
            prep_attn_w(wk_in, 512, wkt, r_wkt, 512, "wk")
            prep_attn_w(wal_in, 512, walt, r_walt, 512, "wal")
            prep_attn_w(wv_in, 1024, wvt, r_wvt, 1024, "wv")
            prep_attn_w(wo_in, 1024, wot, r_wot, 1024, "wo")

        barW = engine_barrier("barW")
        with ExitStack() as ph0f:
            e0 = ph0f.enter_context
            wldF = [e0(nc.sbuf_tensor(f"wldF{i}", [128, I], F32)) for i in range(2)]
            r_wldF = [Res("wldF0"), Res("wldF1")]
            s_wf = [eg(nc.semaphore(f"s_wf{i}")) for i in range(2)]
            wqtm2 = [e0(nc.sbuf_tensor(f"wqtm{i}", [128, I], BF16)) for i in range(2)]
            wqfm2 = [e0(nc.sbuf_tensor(f"wqfm{i}", [128, I], BF16)) for i in range(2)]
            r_wqtm2 = [Res("wqtm0"), Res("wqtm1")]
            r_wqfm2 = [Res("wqfm0"), Res("wqfm1")]
            qs02 = [alloc_qscr(e0, I, f"p0q{i}", alias_k=False) for i in range(2)]
            rqs02 = [Res("qs0a"), Res("qs0b")]

            def prep_ffn_w(src, rows, K, dst_d, rti0, tag, kf32=True):
                nrt = rows // 128
                nkc = K // 128
                for rt in range(nrt):
                    pi = (rti0 + rt) % 2
                    em("sync", lambda e, rt=rt, pi=pi: e.dma_start(out=wldF[pi][:, 0:K], in_=src[rt * 128:(rt + 1) * 128, :]),
                       writes=[r_wldF[pi]], extra=barW, sem=s_wf[pi], inc=16, tag=tag + "ld")
                    quant(wldF[pi][:, 0:K], K, wqtm2[pi][:, 0:K], r_wldF[pi], r_wqtm2[pi],
                          qs02[pi], rqs02[pi], kf32=kf32, tag=tag + "q")
                    for gq in range(0, nkc, 4):
                        ng = min(4, nkc - gq)
                        srcs = [wqtm2[pi][:, (gq + q2) * 128:(gq + q2 + 1) * 128] for q2 in range(ng)]
                        dv = wqfm2[pi][:, gq * 128:(gq + ng) * 128]
                        transpose_tiles(srcs, pC[:, (pi) * 512:(pi + 1) * 512], rC[pi], dv,
                                        r_wqtm2[pi], r_wqfm2[pi], tag=tag + "tr")
                    em("sync", lambda e, rt=rt, pi=pi: e.dma_start(
                        out=dst_d.rearrange("(kc p) r -> p kc r", p=128)[:, :, rt * 128:(rt + 1) * 128],
                        in_=wqfm2[pi][:, 0:nkc * 128].rearrange("p (kc r) -> p kc r", r=128)),
                        reads=[r_wqfm2[pi]], writes=[r_wqd_scr], sem=s_scr, inc=16, tag=tag + "st")
                return rti0 + nrt

            rti = prep_ffn_w(wg_in, I, 1024, wqgt_d, 0, "wg", kf32=False)
            rti = prep_ffn_w(wu_in, I, 1024, wqut_d, rti, "wu", kf32=False)
            rti = prep_ffn_w(wd_in, 1024, I, wqdt_d, rti, "wd", kf32=True)

        bar0 = engine_barrier("bar0")
        bar0.append(r_wqd_scr.writer)
        with ExitStack() as phA:
            eA = phA.enter_context
            xt = [eA(nc.sbuf_tensor(f"xt{i}", [128, 1024], F32)) for i in range(2)]
            xrt = [eA(nc.sbuf_tensor(f"xrt{i}", [128, 1024], F32)) for i in range(2)]
            r_xt = [Res("xt0"), Res("xt1")]
            r_xrt = [Res("xrt0"), Res("xrt1")]
            qsA = alloc_qscr(eA, 1024, "pA")
            junk = qsA["q_absf"]
            lnt = eA(nc.sbuf_tensor("lnt", [128, 8], F32))
            r_lnt = Res("lnt")
            xh_tm = eA(nc.sbuf_tensor("xh_tm", [128, 4 * 1024], BF16))
            xh_fm = eA(nc.sbuf_tensor("xh_fm", [128, 8 * 512], BF16))
            q_fm = eA(nc.sbuf_tensor("q_fm", [128, 4 * 512], BF16))
            qk_fm = eA(nc.sbuf_tensor("qk_fm", [128, 4 * 512], BF16))
            sp_fm = eA(nc.sbuf_tensor("sp_fm", [128, 4 * 512], BF16))
            al_e = eA(nc.sbuf_tensor("al_e", [128, 512], BF16))
            al_l = eA(nc.sbuf_tensor("al_l", [128, 512], BF16))
            al_r = eA(nc.sbuf_tensor("al_r", [128, 512], BF16))
            sgn_v = eA(nc.sbuf_tensor("sgn_v", [128, 512], BF16))
            av_v = qsA["q_absf"]
            vt_t = eA(nc.sbuf_tensor("vt_t", [128, 4 * 1024], BF16))
            a_hm = eA(nc.sbuf_tensor("a_hm", [8, TT], F32))
            b_hm = eA(nc.sbuf_tensor("b_hm", [8, TT], F32))
            y_tm = eA(nc.sbuf_tensor("y_tm", [128, 1024], BF16))
            y_fm = eA(nc.sbuf_tensor("y_fm", [128, 4 * 1024], BF16))
            x2sb = eA(nc.sbuf_tensor("x2sb", [128, 1024], F32))
            h2sb = eA(nc.sbuf_tensor("h2sb", [128, 1024], F32))
            h2q2 = [eA(nc.sbuf_tensor(f"h2q{i}", [128, 1024], BF16)) for i in range(2)]
            h2qf = eA(nc.sbuf_tensor("h2qf", [128, 1024], BF16))
            (r_xhtm, r_xhfm, r_qfm, r_qkfm, r_spfm, r_ale, r_all, r_alr, r_sgnv,
             r_vt, r_ahm, r_bhm, r_ytm, r_yfm, r_x2, r_h2,
             r_h2qf) = (Res(x) for x in
                        ("xh_tm", "xh_fm", "q_fm", "qk_fm", "sp_fm", "al_e", "al_l", "al_r",
                         "sgn_v", "vt_t", "a_hm", "b_hm", "y_tm", "y_fm", "x2",
                         "h2", "h2qf"))
            r_h2q2 = [Res("h2q0"), Res("h2q1")]
            lam = eA(nc.sbuf_tensor("lam", [8, 1], F32))
            Rt = eA(nc.sbuf_tensor("Rt", [8, D], F32))
            Rb = eA(nc.sbuf_tensor("Rb", [8, D], BF16))
            r_cs = eA(nc.sbuf_tensor("r_cs", [8, C], F32))
            w_hm = eA(nc.sbuf_tensor("w_hm", [8, C], F32))
            M_hm = eA(nc.sbuf_tensor("M_hm", [8, C], F32))
            negM = eA(nc.sbuf_tensor("negM", [8, C], F32))
            negM_cat = eA(nc.sbuf_tensor("negM_cat", [1, 1024], F32))
            Emm = eA(nc.sbuf_tensor("Emm", [128, 1024], BF16))
            Ett = qsA["q_u"]
            zc_row = eA(nc.sbuf_tensor("zc_row", [1, 1024], F32))
            zrow = eA(nc.sbuf_tensor("zrow", [1, 1024], F32))
            ciz = eA(nc.sbuf_tensor("ciz", [1, 2048], F32))
            c_row = ciz[0:1, 0:1024]
            izrow = ciz[0:1, 1024:2048]
            cizhm = eA(nc.sbuf_tensor("cizhm", [8, 256], F32))
            c_hm = cizhm[:, 0:128]
            iz_hm = cizhm[:, 128:256]
            c_hmb = eA(nc.sbuf_tensor("c_hmb", [8, C], BF16))
            wcol = eA(nc.sbuf_tensor("wcol", [128, 512], BF16))
            wtot = eA(nc.sbuf_tensor("wtot", [128, 128], BF16))
            lnzt = eA(nc.sbuf_tensor("lnzt", [8, 1], F32))
            (r_lam, r_R, r_Rb, r_rcs, r_whm, r_Mhm, r_negM, r_cat, r_Emm, r_Et,
             r_zc, r_z, r_iz, r_cr, r_chm, r_chmb, r_izhm, r_wcol, r_wtot,
             r_lnz, r_tsc) = (Res(x) for x in
                              ("lam", "R", "Rb", "rcs", "whm", "Mhm", "negM", "cat", "Emm",
                               "Et", "zc", "z", "iz", "cr", "chm", "chmb", "izhm", "wcol",
                               "wtot", "lnz", "tsc"))

            SX, SSQ, MEAN, VAR, RSTD, NMR = (lnt[:, i:i + 1] for i in range(6))

            def ln_emit(src_ap, r_src, dst_ap, r_dst, tag):
                em("vector", lambda e: e.tensor_scalar(junk[:], src_ap, 0.0, 0.0, ALU.add, ALU.add,
                                                       accum_out=SX), reads=[r_src], writes=[r_qs, r_lnt], tag=tag)
                em("vector", lambda e: e.scalar_tensor_tensor(junk[:], src_ap, 1.0, src_ap, ALU.bypass, ALU.mult,
                                                              accum_out=SSQ), reads=[r_src], writes=[r_qs], tag=tag)
                em("vector", lambda e: e.tensor_scalar(MEAN, SX, 1.0 / D, None, ALU.mult), tag=tag)
                em("vector", lambda e: e.tensor_mul(NMR, MEAN, MEAN), tag=tag)
                em("vector", lambda e: e.scalar_tensor_tensor(VAR, SSQ, 1.0 / D, NMR, ALU.mult, ALU.subtract),
                   writes=[r_lnt], tag=tag)
                em("scalar", lambda e: e.activation(VAR, VAR, AF.Ln, bias=c_lneps, scale=1.0),
                   reads=[r_lnt], extra=[t_ld], tag=tag)
                em("scalar", lambda e: e.activation(RSTD, VAR, AF.Exp, bias=0.0, scale=c_neghalf),
                   writes=[r_lnt], tag=tag)
                em("vector", lambda e: e.scalar_tensor_tensor(NMR, MEAN, -1.0, RSTD, ALU.mult, ALU.mult),
                   reads=[r_lnt], writes=[r_lnt], tag=tag)
                return em("scalar", lambda e: e.activation(dst_ap, src_ap, AF.Identity, bias=NMR, scale=RSTD),
                          reads=[r_src, r_lnt], writes=[r_dst], tag=tag + "n")

            pendA = None

            def h2_tail(b, t0, sub):
                h2q = h2q2[sub % 2]
                for g2 in range(2):
                    srcs = [h2q[:, (g2 * 4 + q2) * 128:(g2 * 4 + q2 + 1) * 128] for q2 in range(4)]
                    transpose_tiles(srcs, pC[:, g2 * 512:(g2 + 1) * 512], rC[g2],
                                    h2qf[:, g2 * 512:(g2 + 1) * 512],
                                    r_h2q2[sub % 2], r_h2qf, tag="h2T")
                em("sync", lambda e, b=b, t0=t0, sub=sub: e.dma_start(
                    out=h2qt_d.rearrange("(kc p) t -> p kc t", p=128)[:, :, b * S + t0 + sub * 128: b * S + t0 + (sub + 1) * 128],
                    in_=h2qf[:].rearrange("p (kc r) -> p kc r", r=128)),
                    reads=[r_h2qf], writes=[r_h2d], sem=s_sh2, inc=16, tag="h2st")

            em("vector", lambda e: e.memset(lam[:], -1e30), writes=[r_lam], extra=[t_ld], tag="init")
            em("vector", lambda e: e.memset(Rt[:], 0.0), writes=[r_R], tag="init")
            em("vector", lambda e: e.memset(Rb[:], 0.0), writes=[r_Rb], tag="init")

            for b in range(Bc if stage >= 1 else 0):
                if b > 0:
                    em("vector", lambda e: e.memset(lam[:], -1e30), writes=[r_lam], tag="reinit")
                    em("vector", lambda e: e.memset(Rt[:], 0.0), writes=[r_R], tag="reinit")
                    em("vector", lambda e: e.memset(Rb[:], 0.0), writes=[r_Rb], tag="reinit")
                for T in range(NT):
                    t0 = T * TT
                    for sub in range(NSUB if "a" in sections else 0):
                        ri = sub % 2
                        em("sync", lambda e, b=b, t0=t0, sub=sub, ri=ri: e.dma_start(
                            out=xt[ri][:], in_=x_in[b, t0 + sub * 128:t0 + (sub + 1) * 128, :]),
                            writes=[r_xt[ri]], extra=bar0, sem=s_x[ri], inc=16, tag="xld")
                        ln_emit(xt[ri][:], r_xt[ri],
                                xh_tm[:, sub * 1024:(sub + 1) * 1024], r_xhtm, f"ln1")
                    for sub in range(NSUB if "b" in sections else 0):
                        for g2 in range(2):
                            srcs = [xh_tm[:, sub * 1024 + (g2 * 4 + q2) * 128: sub * 1024 + (g2 * 4 + q2 + 1) * 128]
                                    for q2 in range(4)]
                            dv = xh_fm[:].rearrange("p (kc t) -> p kc t", t=512)[:, g2 * 4:(g2 + 1) * 4,
                                                                                sub * 128:(sub + 1) * 128]
                            transpose_tiles(srcs, pB[:, g2 * 512:(g2 + 1) * 512], rB[g2], dv,
                                            r_xhtm, r_xhfm, tag="xhT")
                    for ncnk in range(4 if "c" in sections else 0):
                        hf = ncnk % 2
                        psA = pA[:, hf * 512:(hf + 1) * 512]
                        mmg([(lambda e, kc=kc, ncnk=ncnk, psA=psA: e.matmul(
                            psA, wqt[:, kc * 512 + ncnk * 128: kc * 512 + (ncnk + 1) * 128],
                            xh_fm[:, kc * 512:(kc + 1) * 512], start=(kc == 0), stop=(kc == 7)))
                            for kc in range(8)], reads=[r_wqt, r_xhfm], writes=[rA[hf]], tag="qmm")
                        em("scalar", lambda e, ncnk=ncnk, psA=psA: e.activation(
                            q_fm[:, ncnk * 512:(ncnk + 1) * 512], psA, AF.Copy, bias=0.0, scale=1.0),
                            reads=[rA[hf]], writes=[r_qfm], tag="qev")
                        mmg([(lambda e, kc=kc, ncnk=ncnk, psA=psA: e.matmul(
                            psA, wkt[:, kc * 512 + ncnk * 128: kc * 512 + (ncnk + 1) * 128],
                            xh_fm[:, kc * 512:(kc + 1) * 512], start=(kc == 0), stop=(kc == 7)))
                            for kc in range(8)], reads=[r_wkt, r_xhfm], writes=[rA[hf]], tag="kmm")
                        em("vector", lambda e, ncnk=ncnk, psA=psA: e.tensor_tensor(
                            qk_fm[:, ncnk * 512:(ncnk + 1) * 512], q_fm[:, ncnk * 512:(ncnk + 1) * 512],
                            psA, ALU.mult), reads=[rA[hf], r_qfm], writes=[r_qkfm], tag="qk")
                        if "g" not in sections:
                            continue
                        psB = pB[:, hf * 512:(hf + 1) * 512]
                        mmg([(lambda e, kc=kc, ncnk=ncnk, psB=psB: e.matmul(
                            psB, walt[:, kc * 512 + ncnk * 128: kc * 512 + (ncnk + 1) * 128],
                            xh_fm[:, kc * 512:(kc + 1) * 512], start=(kc == 0), stop=(kc == 7)))
                            for kc in range(8)], reads=[r_walt, r_xhfm], writes=[rB[hf]], tag="almm")
                        balc = walb[:, ncnk:ncnk + 1]
                        if "1" in sections:
                            em("scalar", lambda e, psB=psB, balc=balc: e.activation(al_e[:], psB, AF.Abs, bias=balc, scale=1.0),
                               reads=[rB[hf], r_const], writes=[r_ale], tag="spabs")
                        if "2" in sections:
                            em("scalar", lambda e: e.activation(al_e[:], al_e[:], AF.Exp, bias=0.0, scale=c_negone),
                               reads=[r_ale], writes=[r_ale], tag="spexp")
                        if "3" in sections:
                            em("scalar", lambda e: e.activation(al_l[:], al_e[:], AF.Ln, bias=1.0, scale=1.0),
                               reads=[r_ale], writes=[r_all], tag="spln")
                        if "4" in sections:
                            em("scalar", lambda e, psB=psB, balc=balc: e.activation(al_r[:], psB, AF.Relu, bias=balc, scale=1.0),
                               reads=[rB[hf], r_const], writes=[r_alr], tag="sprelu")
                        if "5" in sections:
                            em("vector", lambda e, ncnk=ncnk: e.tensor_add(
                                sp_fm[:, ncnk * 512:(ncnk + 1) * 512], al_l[:], al_r[:]),
                                reads=[r_all, r_alr], writes=[r_spfm], tag="sp")
                    if "h" not in sections:
                        continue
                    mmg([(lambda e, kc=kc: e.matmul(pN[:, 0:512], maskb[:, kc * 8:(kc + 1) * 8],
                                                    qk_fm[:, kc * 512:(kc + 1) * 512],
                                                    start=(kc == 0), stop=(kc == 3))) for kc in range(4)],
                        reads=[r_const, r_qkfm], writes=[rN[0]], tag="bmm")
                    mmg([(lambda e, kc=kc: e.matmul(pN[:, 512:1024], maska[:, kc * 8:(kc + 1) * 8],
                                                    sp_fm[:, kc * 512:(kc + 1) * 512],
                                                    start=(kc == 0), stop=(kc == 3))) for kc in range(4)],
                        reads=[r_const, r_spfm], writes=[rN[1]], tag="amm")
                    em("vector", lambda e: e.tensor_copy(b_hm[:], pN[:, 0:512]),
                       reads=[rN[0]], writes=[r_bhm], tag="bev")
                    em("vector", lambda e: e.tensor_copy(a_hm[:], pN[:, 512:1024]),
                       reads=[rN[1]], writes=[r_ahm], tag="aev")
                    if T == 0:
                        em("vector", lambda e: e.memset(a_hm[:, 0:1], 0.0), writes=[r_ahm], tag="a0")
                    for sub in range(NSUB if "d" in sections else 0):
                        for hf in range(2):
                            psC = pC[:, hf * 512:(hf + 1) * 512]
                            mmg([(lambda e, kc=kc, sub=sub, hf=hf, psC=psC: e.matmul(
                                psC, xh_fm[:, kc * 512 + sub * 128: kc * 512 + (sub + 1) * 128],
                                wvt[:, kc * 1024 + hf * 512: kc * 1024 + (hf + 1) * 512],
                                start=(kc == 0), stop=(kc == 7))) for kc in range(8)],
                                reads=[r_xhfm, r_wvt], writes=[rC[hf]], tag="vmm")
                            em("scalar", lambda e, psC=psC: e.activation(sgn_v[:], psC, AF.Sign, bias=0.0, scale=1.0),
                               reads=[rC[hf]], writes=[r_sgnv], tag="vsgn")
                            em("scalar", lambda e, psC=psC: e.activation(av_v[:, 0:512], psC, AF.Abs, bias=0.0, scale=1.0),
                               reads=[rC[hf]], writes=[r_qs], tag="vabs")
                            em("vector", lambda e: e.tensor_scalar(av_v[:, 0:512], av_v[:, 0:512], 1.0 + EPS, EPS * (1.0 + EPS),
                                                                   ALU.mult, ALU.add),
                               reads=[r_qs], writes=[r_qs], tag="veps")
                            if T == 0 and sub == 0:
                                em("vector", lambda e: e.tensor_scalar(av_v[0:1, 0:512], av_v[0:1, 0:512],
                                                                       1.0 / (1.0 + EPS), 0.0, ALU.mult, ALU.add),
                                   reads=[r_qs], writes=[r_qs], tag="veps0")
                            em("vector", lambda e, sub=sub, hf=hf: e.tensor_mul(
                                vt_t[:, sub * 1024 + hf * 512: sub * 1024 + (hf + 1) * 512], av_v[:, 0:512], sgn_v[:]),
                                reads=[r_qs, r_sgnv], writes=[r_vt], tag="vt")
                    for jj in range(CPT if "e" in sections else 0):
                        sl = slice(jj * C, (jj + 1) * C)
                        em("vector", lambda e, sl=sl: e.tensor_tensor_scan(r_cs[:], a_hm[:, sl], a_hm[:, sl],
                                                                           0.0, ALU.add, ALU.bypass),
                           reads=[r_ahm], writes=[r_rcs], tag="r")
                        em("vector", lambda e, sl=sl: e.tensor_sub(w_hm[:], b_hm[:, sl], r_cs[:]),
                           reads=[r_bhm, r_rcs], writes=[r_whm], tag="w")
                        em("vector", lambda e: e.tensor_tensor_scan(M_hm[:], w_hm[:], w_hm[:], -1e30, ALU.max, ALU.max),
                           reads=[r_whm], writes=[r_Mhm], tag="M")
                        em("vector", lambda e: e.tensor_scalar(negM[:], M_hm[:], -1.0, None, ALU.mult),
                           reads=[r_Mhm], writes=[r_negM], tag="negm")
                        em("scalar", lambda e: e.dma_start(
                            out=negM_cat[0:1, :].rearrange("p (hh c) -> p hh c", hh=8), in_=negM[:]),
                            reads=[r_negM], writes=[r_cat], sem=s_cat, inc=16, tag="cat")
                        for hf in range(2):
                            hs = slice(hf * 512, (hf + 1) * 512)
                            mmg([
                                (lambda e, hs=hs: e.matmul(pA[:, hs], w_hm[:], s8[:, hs], start=True, stop=False)),
                                (lambda e, hs=hs: e.matmul(pA[:, hs], ones[0:1, :], negM_cat[0:1, hs], start=False, stop=False)),
                                (lambda e, hs=hs: e.matmul(pA[:, hs], tribig[:], itile[:, hs], start=False, stop=True)),
                            ], reads=[r_whm, r_cat, r_const], writes=[rA[hf]], tag="Dmm")
                        em("scalar", lambda e: e.activation(Emm[:], pA[:], AF.Exp, bias=0.0, scale=1.0),
                           reads=[rA[0], rA[1]], writes=[r_Emm], tag="E")
                        for hf in range(2):
                            hs = slice(hf * 512, (hf + 1) * 512)
                            mmg([
                                (lambda e, hs=hs: e.matmul(pB[0:1, hs], lam[:], s8[:, hs], start=True, stop=False)),
                                (lambda e, hs=hs: e.matmul(pB[0:1, hs], ones[0:1, 0:1], negM_cat[0:1, hs], start=False, stop=True)),
                            ], reads=[r_lam, r_cat, r_const], writes=[rB[hf]], tag="zmm")
                        em("scalar", lambda e: e.activation(zc_row[:], pB[0:1, :], AF.Exp, bias=0.0, scale=1.0),
                           reads=[rB[0], rB[1]], writes=[r_zc], tag="zc")
                        for hf in range(2):
                            hs = slice(hf * 512, (hf + 1) * 512)
                            mmg([(lambda e, hs=hs: e.matmul(pB[0:1, hs], onesb[:], Emm[:, hs], start=True, stop=True))],
                                reads=[r_Emm, r_const, r_zc], writes=[rB[hf]], tag="cs")
                        em("vector", lambda e: e.tensor_add(zrow[:], zc_row[:], pB[0:1, :]),
                           reads=[r_zc, rB[0], rB[1]], writes=[r_z], tag="z")
                        em("scalar", lambda e: e.activation(zrow[:], zrow[:], AF.Ln, bias=0.0, scale=1.0),
                           reads=[r_z], writes=[r_z], tag="izln")
                        em("scalar", lambda e: e.activation(izrow, zrow[:], AF.Exp, bias=0.0, scale=cst[0:1, 2:3]),
                           reads=[r_z, r_const], writes=[r_iz], tag="izr")
                        em("vector", lambda e: e.tensor_mul(c_row, zc_row[:], izrow),
                           reads=[r_zc, r_iz], writes=[r_cr], tag="cr")
                        em("scalar", lambda e: e.dma_start(
                            out=cizhm[:, 0:128], in_=ciz[0:1, 0:1024].rearrange("p (hh c) -> p hh c", hh=8)),
                            reads=[r_cr], writes=[r_chm], sem=s_c, inc=16, tag="chm")
                        em("scalar", lambda e: e.dma_start(
                            out=cizhm[:, 128:256], in_=ciz[0:1, 1024:2048].rearrange("p (hh c) -> p hh c", hh=8)),
                            reads=[r_iz], writes=[r_izhm], sem=s_iz, inc=16, tag="izhm")
                        em("vector", lambda e: e.tensor_scalar(c_hmb[:], c_hm, 1.0, None, ALU.mult),
                           reads=[r_chm], writes=[r_chmb], tag="chb")
                        for hf in range(2):
                            hs = slice(hf * 512, (hf + 1) * 512)
                            mmg([(lambda e, hs=hs: e.matmul(pB[:, hs], ones[0:1, :], izrow[0:1, hs], start=True, stop=True))],
                                reads=[r_iz, r_const], writes=[rB[hf]], tag="ib")
                        em("vector", lambda e: e.tensor_mul(Ett[:], Emm[:], pB[:]),
                           reads=[r_Emm, rB[0], rB[1]], writes=[r_qs], tag="Et")
                        em("vector", lambda e: e.tensor_add(wcol[:], Ett[:, 0:512], Ett[:, 512:1024]),
                           reads=[r_qs], writes=[r_wcol], tag="c1")
                        em("vector", lambda e: e.tensor_add(wcol[:, 0:256], wcol[:, 0:256], wcol[:, 256:512]),
                           reads=[r_wcol], writes=[r_wcol], tag="c2")
                        em("vector", lambda e: e.tensor_add(wtot[:], wcol[:, 0:128], wcol[:, 128:256]),
                           reads=[r_wcol], writes=[r_wtot], tag="wtot")
                        for hf in range(2):
                            hs = slice(hf * 512, (hf + 1) * 512)
                            vsl = slice(jj * 1024 + hf * 512, jj * 1024 + (hf + 1) * 512)
                            mmg([
                                (lambda e, hs=hs, vsl=vsl: e.matmul(pA[:, hs], wtot[:], vt_t[:, vsl], start=True, stop=False)),
                                (lambda e, hs=hs: e.matmul(pA[:, hs], c_hmb[:], Rb[:, hs], start=False, stop=True)),
                            ], reads=[r_wtot, r_vt, r_chmb, r_Rb], writes=[rA[hf]], tag="Ymm")
                        Esl = Emm[:].rearrange("p (hh c) -> p hh c", hh=8)[:, :, 127:128].squeeze(2)
                        for hf in range(2):
                            hs = slice(hf * 512, (hf + 1) * 512)
                            vsl = slice(jj * 1024 + hf * 512, jj * 1024 + (hf + 1) * 512)
                            mmg([(lambda e, hs=hs, vsl=vsl: e.matmul(pN[:, hs], Esl, vt_t[:, vsl], start=True, stop=True))],
                                reads=[r_Emm, r_vt], writes=[rN[hf]], tag="NE")
                        em("scalar", lambda e: e.activation(y_tm[:], pA[:], AF.Copy, bias=0.0, scale=1.0),
                           reads=[rA[0], rA[1]], writes=[r_ytm], tag="yev")
                        for g2 in range(2):
                            srcs = [y_tm[:, (g2 * 4 + q2) * 128:(g2 * 4 + q2 + 1) * 128] for q2 in range(4)]
                            dv = y_fm[:].rearrange("p (kc t) -> p kc t", t=512)[:, g2 * 4:(g2 + 1) * 4,
                                                                               jj * 128:(jj + 1) * 128]
                            transpose_tiles(srcs, pC[:, g2 * 512:(g2 + 1) * 512], rC[g2], dv,
                                            r_ytm, r_yfm, tag="yT")
                        em("vector", lambda e: e.tensor_scalar(Rt[:], Rt[:], cizhm[:, 127:128], None, ALU.mult),
                           reads=[r_R, r_chm], writes=[r_R], tag="Rdec")
                        em("vector", lambda e: e.scalar_tensor_tensor(Rt[:], pN[:], cizhm[:, 255:256], Rt[:],
                                                                      ALU.mult, ALU.add),
                           reads=[rN[0], rN[1], r_izhm, r_R], writes=[r_R], tag="R")
                        em("vector", lambda e: e.tensor_scalar(Rb[:], Rt[:], 1.0, None, ALU.mult),
                           reads=[r_R], writes=[r_Rb], tag="Rbc")
                        em("scalar", lambda e: e.activation(lnzt[:], cizhm[:, 255:256], AF.Ln, bias=0.0, scale=1.0),
                           reads=[r_izhm], writes=[r_lnz], tag="lnu")
                        em("vector", lambda e: e.tensor_sub(lam[:], M_hm[:, 127:128], lnzt[:]),
                           reads=[r_Mhm, r_lnz], writes=[r_lam], tag="lam1")
                        em("vector", lambda e: e.tensor_add(lam[:], lam[:], r_cs[:, 127:128]),
                           reads=[r_rcs, r_lam], writes=[r_lam], tag="lam2")
                    for sub in range(NSUB if "f" in sections else 0):
                        ri = sub % 2
                        em("sync", lambda e, b=b, t0=t0, sub=sub, ri=ri: e.dma_start(
                            out=xrt[ri][:], in_=x_in[b, t0 + sub * 128:t0 + (sub + 1) * 128, :]),
                            writes=[r_xrt[ri]], extra=bar0, sem=s_xr[ri], inc=16, tag="xrld")
                        for hf in range(2):
                            psC = pC[:, hf * 512:(hf + 1) * 512]
                            mmg([(lambda e, kc=kc, sub=sub, hf=hf, psC=psC: e.matmul(
                                psC, y_fm[:, kc * 512 + sub * 128: kc * 512 + (sub + 1) * 128],
                                wot[:, kc * 1024 + hf * 512: kc * 1024 + (hf + 1) * 512],
                                start=(kc == 0), stop=(kc == 7))) for kc in range(8)],
                                reads=[r_yfm, r_wot], writes=[rC[hf]], tag="omm")
                            em("vector", lambda e, hf=hf, psC=psC, ri=ri: e.tensor_add(
                                x2sb[:, hf * 512:(hf + 1) * 512], psC, xrt[ri][:, hf * 512:(hf + 1) * 512]),
                                reads=[rC[hf], r_xrt[ri]], writes=[r_x2] if hf == 1 else [], tag="x2")
                        em("sync", lambda e, b=b, t0=t0, sub=sub: e.dma_start(
                            out=x2_d[b * S + t0 + sub * 128: b * S + t0 + (sub + 1) * 128, :], in_=x2sb[:]),
                            reads=[r_x2], writes=[r_x2d], sem=s_sx2, inc=16, tag="x2st")
                        ln_emit(x2sb[:], r_x2, h2sb[:], r_h2, "ln2")
                        quant(h2sb[:], 1024, h2q2[sub % 2][:], r_h2, r_h2q2[sub % 2], qsA, tag="h2q")
                        if pendA is not None:
                            h2_tail(*pendA)
                        pendA = (b, t0, sub)

            if pendA is not None:
                h2_tail(*pendA)
        barA = engine_barrier("barA")
        barA += [r_x2d.writer, r_h2d.writer]
        ph0A.close()
        if stage < 2:
            n_tail_sem, n_tail = (s_scr, P.scnt.get(id(s_scr), 0)) if stage == 0 else (s_sh2, P.scnt.get(id(s_sh2), 0))
            block = eg(nc.Block())
            P.finalize(block, tail={"sync": lambda eng: eng.wait_ge(n_tail_sem, n_tail)})
            return nc
        with ExitStack() as phB:
            eB = phB.enter_context
            wqg = eB(nc.sbuf_tensor("wqg", [128, 8 * I], BF16))
            wqu = eB(nc.sbuf_tensor("wqu", [128, 8 * I], BF16))
            wqd = eB(nc.sbuf_tensor("wqd", [128, 22 * 1024], BF16))
            x2t = [eB(nc.sbuf_tensor(f"x2t{i}", [128, 1024], F32)) for i in range(2)]
            h2f = [eB(nc.sbuf_tensor("h2f0", [128, 8 * 128], BF16))] * 2
            hm_tm = eB(nc.sbuf_tensor("hm_tm", [128, I], F32))
            hmq2 = [eB(nc.sbuf_tensor(f"hmq{i}", [128, I], BF16)) for i in range(2)]
            qsB = alloc_qscr(eB, I, "pB")
            hmqf = eB(nc.sbuf_tensor("hmqf", [128, 22 * 128], BF16))
            r_wqg, r_wqu, r_wqd = Res("wqg"), Res("wqu"), Res("wqd")
            r_x2t = [Res("x2t0"), Res("x2t1")]
            _rh = Res("h2f0")
            r_h2f = [_rh, _rh]
            r_hm = Res("hm")
            r_hmq2 = [Res("hmq0"), Res("hmq1")]
            r_hmqf = Res("hmqf")

            em("sync", lambda e: e.dma_start(
                out=wqg[:].rearrange("p (kc r) -> p kc r", r=I),
                in_=wqgt_d.rearrange("(kc p) r -> p kc r", p=128)),
                reads=[r_wqd_scr], writes=[r_wqg], extra=barA, sem=s_w, inc=16, tag="wgld")
            em("sync", lambda e: e.dma_start(
                out=wqu[:].rearrange("p (kc r) -> p kc r", r=I),
                in_=wqut_d.rearrange("(kc p) r -> p kc r", p=128)),
                reads=[r_wqd_scr], writes=[r_wqu], extra=barA, sem=s_w, inc=16, tag="wuld")
            t_wd = em("sync", lambda e: e.dma_start(
                out=wqd[:].rearrange("p (kc r) -> p kc r", r=1024),
                in_=wqdt_d.rearrange("(kc p) r -> p kc r", p=128)),
                reads=[r_wqd_scr], writes=[r_wqd], extra=barA, sem=s_w, inc=16, tag="wdld")
            r_wqg.writer = t_wd
            r_wqu.writer = t_wd

            NSl = [(i * 512, min(I, (i + 1) * 512)) for i in range((I + 511) // 512)]

            def sub_head(b, t0, sub, rx):
                hmq = hmq2[sub % 2]
                for gq in range(0, 22, 4):
                    ng = min(4, 22 - gq)
                    srcs = [hmq[:, (gq + q2) * 128:(gq + q2 + 1) * 128] for q2 in range(ng)]
                    transpose_tiles(srcs, pC[:, 0:512], rC[0],
                                    hmqf[:, gq * 128:(gq + ng) * 128],
                                    r_hmq2[sub % 2], r_hmqf, tag="hmT")

            def sub_tail(b, t0, sub, rx):
                em("sync", lambda e, b=b, t0=t0, sub=sub, rx=rx: e.dma_start(
                    out=x2t[rx][:], in_=x2_d[b * S + t0 + sub * 128: b * S + t0 + (sub + 1) * 128, :]),
                    reads=[r_x2d], writes=[r_x2t[rx]], extra=barA, sem=s_x[rx], inc=16, tag="x2ld")
                outsb = x2t[rx]
                for hf in range(2):
                    mmg([(lambda e, kc=kc, hf=hf: e.matmul(
                        pC[:, 512:1024], hmqf[:, kc * 128:(kc + 1) * 128],
                        wqd[:, kc * 1024 + hf * 512: kc * 1024 + (hf + 1) * 512],
                        start=(kc == 0), stop=(kc == 21))) for kc in range(22)],
                        reads=[r_hmqf, r_wqd], writes=[rC[1]], tag="dmm")
                    em("vector", lambda e, hf=hf, rx=rx: e.tensor_add(
                        outsb[:, hf * 512:(hf + 1) * 512], pC[:, 512:1024],
                        x2t[rx][:, hf * 512:(hf + 1) * 512]),
                        reads=[rC[1]], writes=[r_x2t[rx]] if hf == 1 else [], tag="oadd")
                em("sync", lambda e, b=b, t0=t0, sub=sub, rx=rx: e.dma_start(
                    out=out_d[b, t0 + sub * 128: t0 + (sub + 1) * 128, :], in_=x2t[rx][:]),
                    reads=[r_x2t[rx]], sem=s_out, inc=16, tag="outdma")

            pend = None
            pend2 = None
            for b in range(Bc):
                for T in range(NT):
                    t0 = T * TT
                    ti = b * NT + T
                    for sub in range(NSUB):
                        rx = sub % 2
                        ri = sub % 2
                        em("sync", lambda e, b=b, t0=t0, sub=sub, ri=ri: e.dma_start(
                            out=h2f[ri][:].rearrange("p (kc t) -> p kc t", t=128),
                            in_=h2qt_d.rearrange("(kc p) t -> p kc t", p=128)[:, :, b * S + t0 + sub * 128: b * S + t0 + (sub + 1) * 128]),
                            reads=[r_h2d], writes=[r_h2f[ri]], extra=barA, sem=s_fl[ri], inc=16, tag="h2ld")
                        for si, (n0, n1) in enumerate(NSl):
                            nn = n1 - n0
                            pg = pA[:, (si % 2) * 512:(si % 2) * 512 + nn]
                            pu = pB[:, (si % 2) * 512:(si % 2) * 512 + nn]
                            mmg([(lambda e, kc=kc, n0=n0, n1=n1, pg=pg, ri=ri: e.matmul(
                                pg, h2f[ri][:, kc * 128:(kc + 1) * 128],
                                wqg[:, kc * I + n0: kc * I + n1], start=(kc == 0), stop=(kc == 7)))
                                for kc in range(8)], reads=[r_h2f[ri], r_wqg], writes=[rA[si % 2]], tag="gmm")
                            em("scalar", lambda e, pg=pg, n0=n0, n1=n1: e.activation(
                                hm_tm[:, n0:n1], pg, AF.Silu, bias=0.0, scale=1.0),
                               reads=[rA[si % 2]], writes=[r_hm], tag="silu")
                            mmg([(lambda e, kc=kc, n0=n0, n1=n1, pu=pu, ri=ri: e.matmul(
                                pu, h2f[ri][:, kc * 128:(kc + 1) * 128],
                                wqu[:, kc * I + n0: kc * I + n1], start=(kc == 0), stop=(kc == 7)))
                                for kc in range(8)], reads=[r_h2f[ri], r_wqu], writes=[rB[si % 2]], tag="umm")
                            em("vector", lambda e, pu=pu, n0=n0, n1=n1: e.tensor_tensor(
                                hm_tm[:, n0:n1], hm_tm[:, n0:n1], pu, ALU.mult),
                                reads=[rB[si % 2], r_hm], writes=[r_hm], tag="hmid")
                        quant(hm_tm[:], I, hmq2[sub % 2][:], r_hm, r_hmq2[sub % 2], qsB, tag="hmq")
                        if pend2 is not None:
                            sub_tail(*pend2)
                        if pend is not None:
                            sub_head(*pend)
                        pend2 = pend
                        pend = (b, t0, sub, rx)
            sub_tail(*pend2)
            sub_head(*pend)
            sub_tail(*pend)

            n_out_total = P.scnt[id(s_out)]

        block = eg(nc.Block())
        P.finalize(block, tail={"sync": lambda eng: eng.wait_ge(s_out, n_out_total)})
        global _LAST_PROG, _LAST_SEMS
        _LAST_PROG = P
        _LAST_SEMS = {"s_pe": s_pe, "s_dve": s_dve, "s_act": s_act,
                      "s_c": s_c, "s_iz": s_iz, "s_cat": s_cat}
    return nc


def kernel(L0_attn_x, L0_attn_W_q, L0_attn_W_k, L0_attn_W_v, L0_attn_W_alpha_w,
           L0_attn_W_alpha_b, L0_attn_W_o, L0_ffn_w_gate, L0_ffn_w_up, L0_ffn_w_down):
    n = 8
    Bc = 2
    x = np.ascontiguousarray(L0_attn_x, dtype=np.float32)
    hc = host_consts()
    walb = np.ascontiguousarray(np.asarray(L0_attn_W_alpha_b, np.float32).reshape(4, 128).T)
    common = dict(
        wq_in=np.ascontiguousarray(L0_attn_W_q, np.float32),
        wk_in=np.ascontiguousarray(L0_attn_W_k, np.float32),
        wal_in=np.ascontiguousarray(L0_attn_W_alpha_w, np.float32),
        wv_in=np.ascontiguousarray(L0_attn_W_v, np.float32),
        wo_in=np.ascontiguousarray(L0_attn_W_o, np.float32),
        walb_in=walb,
        wg_in=np.ascontiguousarray(L0_ffn_w_gate, np.float32),
        wu_in=np.ascontiguousarray(L0_ffn_w_up, np.float32),
        wd_in=np.ascontiguousarray(L0_ffn_w_down, np.float32),
        **{k: np.ascontiguousarray(v) for k, v in hc.items()},
    )
    in_maps = []
    for c in range(n):
        m = dict(common)
        m["x_in"] = np.ascontiguousarray(x[c * Bc:(c + 1) * Bc])
        in_maps.append(m)
    nc = build_core(Bc=Bc, S=2048)
    res = run_bass_kernel_spmd(nc, in_maps, core_ids=list(range(n)))
    out = np.concatenate([np.asarray(r["out"], np.float32) for r in res.results], axis=0)
    return out


# version 15
# speedup vs baseline: 1.0947x; 1.0300x over previous
import numpy as np
import ml_dtypes
import concourse.bass as bass
import concourse.mybir as mybir
from concourse.bass_utils import run_bass_kernel_spmd
from contextlib import ExitStack

F32 = mybir.dt.float32
BF16 = mybir.dt.bfloat16
AF = mybir.ActivationFunctionType
ALU = mybir.AluOpType
AX = mybir.AxisListType

D = 1024
H = 8
I = 2816
C = 128
TT = 512
EPS = 1e-8
BF = ml_dtypes.bfloat16


class Tok:
    __slots__ = ("eng", "seq", "sem", "val")
    def __init__(self, eng, seq, sem, val):
        self.eng, self.seq, self.sem, self.val = eng, seq, sem, val


class Res:
    __slots__ = ("writer", "readers", "name")
    def __init__(self, name=""):
        self.writer = None
        self.readers = []
        self.name = name


class Prog:
    ENGS = ("tensor", "vector", "scalar", "gpsimd", "sync")

    def __init__(self, nc, sems):
        self.nc = nc
        self.ops = {e: [] for e in self.ENGS}
        self.scnt = {}
        self.watermark = {e: {} for e in self.ENGS}
        self.engsem = sems

    def raw_emit(self, eng, fn, deps, sem, inc, tag):
        waits = {}
        for d in deps:
            if d is None or d.eng == eng:
                continue
            sid = id(d.sem)
            if waits.get(sid, (None, 0))[1] < d.val:
                waits[sid] = (d.sem, d.val)
        todo = []
        wm = self.watermark[eng]
        for sid, (s, v) in waits.items():
            if wm.get(sid, 0) < v:
                todo.append((s, v))
                wm[sid] = v
        if inc:
            self.scnt[id(sem)] = self.scnt.get(id(sem), 0) + inc
            tok = Tok(eng, len(self.ops[eng]), sem, self.scnt[id(sem)])
        else:
            tok = Tok(eng, len(self.ops[eng]), None, None)
        self.ops[eng].append((todo, fn, sem if inc else None, inc, tag))
        return tok

    def finalize(self, block, tail=None):
        def make_body(eng_name):
            def body(eng):
                for todo, fn, sem, inc, tag in self.ops[eng_name]:
                    for (s, v) in todo:
                        eng.wait_ge(s, v)
                    ins = fn(eng)
                    if inc:
                        ins.then_inc(sem, inc)
                if tail and eng_name in tail:
                    tail[eng_name](eng)
            return body
        for e in ("sync", "vector", "scalar", "tensor", "gpsimd"):
            getattr(block, e)(make_body(e))


def host_consts():
    s8 = np.zeros((8, 1024), np.float32)
    for h in range(8):
        s8[h, h * 128:(h + 1) * 128] = 1.0
    tribig = np.where(np.arange(128)[None, :] > np.arange(128)[:, None], -1e30, 0.0).astype(np.float32)
    itile = np.tile(np.eye(128, dtype=np.float32), (1, 8))
    ones = np.ones((128, 128), np.float32)
    onesb = np.ones((128, 1), BF)
    i128b = np.eye(128, dtype=np.float32).astype(BF)
    maskb = np.zeros((128, 32), np.float32)
    maska = np.zeros((128, 32), np.float32)
    for kc in range(4):
        for r in range(128):
            h = (kc * 128 + r) // 64
            maskb[r, kc * 8 + h] = 0.125
            maska[r, kc * 8 + h] = -1.0
    cvals = np.array([-0.5, 1e-5, -1.0, 1.0 / 1024.0, 0.0], np.float32)
    cst = np.tile(cvals[None, :], (128, 1)).astype(np.float32)
    return dict(c_s8=s8, c_tribig=tribig, c_itile=itile, c_ones=ones,
                c_onesb=onesb, c_i128b=i128b, c_maskb=maskb.astype(BF),
                c_maska=maska.astype(BF), c_cst=cst)


def build_core(Bc=2, S=2048, stage=2, sections="abcdefgh12345"):
    NCH = S // C
    NT = S // TT
    CPT = TT // C
    NSUB = TT // 128

    nc = bass.Bass("TRN2", target_bir_lowering=False, detect_race_conditions=False)
    x_in = nc.dram_tensor("x_in", [Bc, S, D], F32, kind="ExternalInput").ap()
    wq_in = nc.dram_tensor("wq_in", [512, 1024], F32, kind="ExternalInput").ap()
    wk_in = nc.dram_tensor("wk_in", [512, 1024], F32, kind="ExternalInput").ap()
    wal_in = nc.dram_tensor("wal_in", [512, 1024], F32, kind="ExternalInput").ap()
    wv_in = nc.dram_tensor("wv_in", [1024, 1024], F32, kind="ExternalInput").ap()
    wo_in = nc.dram_tensor("wo_in", [1024, 1024], F32, kind="ExternalInput").ap()
    walb_in = nc.dram_tensor("walb_in", [128, 4], F32, kind="ExternalInput").ap()
    wg_in = nc.dram_tensor("wg_in", [I, 1024], F32, kind="ExternalInput").ap()
    wu_in = nc.dram_tensor("wu_in", [I, 1024], F32, kind="ExternalInput").ap()
    wd_in = nc.dram_tensor("wd_in", [1024, I], F32, kind="ExternalInput").ap()
    hc = host_consts()
    cin = {}
    for k, v in hc.items():
        dt = BF16 if v.dtype == BF else F32
        cin[k] = nc.dram_tensor(k, list(v.shape), dt, kind="ExternalInput").ap()
    out_d = nc.dram_tensor("out", [Bc, S, D], F32, kind="ExternalOutput").ap()
    x2_d = nc.dram_tensor("x2_d", [Bc * S, D], F32, kind="Internal").ap()
    h2qt_d = nc.dram_tensor("h2qt_d", [D, Bc * S], BF16, kind="Internal").ap()
    wqgt_d = nc.dram_tensor("wqgt_d", [1024, I], BF16, kind="Internal").ap()
    wqut_d = nc.dram_tensor("wqut_d", [1024, I], BF16, kind="Internal").ap()
    wqdt_d = nc.dram_tensor("wqdt_d", [I, 1024], BF16, kind="Internal").ap()

    with ExitStack() as g:
        eg = g.enter_context
        s_pe = eg(nc.semaphore("s_pe"))
        s_dve = eg(nc.semaphore("s_dve"))
        s_act = eg(nc.semaphore("s_act"))
        s_gp = eg(nc.semaphore("s_gp"))
        s_const = eg(nc.semaphore("s_const"))
        s_w = eg(nc.semaphore("s_w"))
        s_x = [eg(nc.semaphore(f"s_x{i}")) for i in range(2)]
        s_xr = [eg(nc.semaphore(f"s_xr{i}")) for i in range(2)]
        s_cat = eg(nc.semaphore("s_cat"))
        s_c = eg(nc.semaphore("s_c"))
        s_iz = eg(nc.semaphore("s_iz"))
        s_scr = eg(nc.semaphore("s_scr"))
        s_sx2 = eg(nc.semaphore("s_sx2"))
        s_sh2 = eg(nc.semaphore("s_sh2"))
        s_fl = [eg(nc.semaphore(f"s_fl{i}")) for i in range(2)]
        s_out = eg(nc.semaphore("s_out"))

        P = Prog(nc, {"tensor": s_pe, "vector": s_dve, "scalar": s_act,
                      "gpsimd": s_gp, "sync": s_const})

        def em(eng, fn, reads=(), writes=(), extra=(), sem=None, inc=1, tag=""):
            deps = list(extra)
            for r in reads:
                deps.append(r.writer)
            for w in writes:
                deps.append(w.writer)
                deps.extend(w.readers)
            tok = P.raw_emit(eng, fn, deps, sem or P.engsem[eng], inc, tag)
            for r in reads:
                r.readers.append(tok)
            for w in writes:
                w.writer = tok
                w.readers = []
            return tok

        def mmg(parts, reads=(), writes=(), extra=(), tag="mm"):
            deps = list(extra)
            for r in reads:
                deps.append(r.writer)
            for w in writes:
                deps.append(w.writer)
                deps.extend(w.readers)
            tok = None
            n = len(parts)
            for i, fn in enumerate(parts):
                tok = P.raw_emit("tensor", fn, deps if i == 0 else (), s_pe,
                                 1 if i == n - 1 else 0, tag)
            for r in reads:
                r.readers.append(tok)
            for w in writes:
                w.writer = tok
                w.readers = []
            return tok

        pA = eg(nc.psum_tensor("pA", [128, 1024], F32))
        pB = eg(nc.psum_tensor("pB", [128, 1024], F32))
        pC = eg(nc.psum_tensor("pC", [128, 1024], F32))
        pN = eg(nc.psum_tensor("pN", [128, 1024], F32))
        rA = [Res("pA0"), Res("pA1")]
        rB = [Res("pB0"), Res("pB1")]
        rC = [Res("pC0"), Res("pC1")]
        rN = [Res("pN0"), Res("pN1")]

        i128b = eg(nc.sbuf_tensor("i128b", [128, 128], BF16))
        cst = eg(nc.sbuf_tensor("cst", [128, 6], F32))
        dummy = eg(nc.sbuf_tensor("barbuf", [1, 8], F32))
        walb = eg(nc.sbuf_tensor("walb", [128, 4], F32))
        r_dummy = Res("dummy")

        def engine_barrier(tagb):
            tv = em("vector", lambda e: e.memset(dummy[:, 0:4], 0.0), writes=[r_dummy], tag=tagb)
            ta = em("scalar", lambda e: e.activation(dummy[:, 4:8], dummy[:, 0:4], AF.Identity, bias=0.0, scale=1.0),
                    reads=[r_dummy], tag=tagb)
            tp = mmg([(lambda e: e.matmul(pN[0:1, 0:16], ones[0:1, 0:1], ones[0:1, 0:16], start=True, stop=True))],
                     reads=[r_const], writes=[rN[0]], tag=tagb)
            return [tv, ta, tp]
        r_qs = Res("qscratch")

        def alloc_qscr(ee, KW, pfx, alias_k=True):
            d_u = ee(nc.sbuf_tensor(pfx + "q_u", [128, KW], BF16))
            return dict(
                q_absf=ee(nc.sbuf_tensor(pfx + "q_absf", [128, KW], F32)),
                q_u=d_u,
                q_h=ee(nc.sbuf_tensor(pfx + "q_h", [128, KW], BF16)),
                q_k=d_u if alias_k else ee(nc.sbuf_tensor(pfx + "q_k", [128, KW], BF16)),
                q_sgn=ee(nc.sbuf_tensor(pfx + "q_sgn", [128, KW], BF16)),
                q_scl=ee(nc.sbuf_tensor(pfx + "q_scl", [128, KW // 16], F32)),
                q_inv=ee(nc.sbuf_tensor(pfx + "q_inv", [128, KW // 16], F32)),
            )

        r_wqt, r_wkt, r_walt, r_wvt, r_wot = (Res(x) for x in ("wqt", "wkt", "walt", "wvt", "wot"))
        r_const = Res("const")
        r_x2d = Res("x2_d")
        r_h2d = Res("h2qt_d")
        r_wqd_scr = Res("wq_scr")

        c_neghalf = cst[:, 0:1]
        c_lneps = cst[:, 1:2]
        c_negone = cst[:, 2:3]

        def quant(x_bf, K, out_bf, r_in, r_out, qs, rqs=None, kf32=True, tag="q"):
            if rqs is None:
                rqs = r_qs
            nb = K // 16
            q_absf, q_u, q_h, q_k, q_sgn, q_scl, q_inv = (
                qs["q_absf"], qs["q_u"], qs["q_h"], qs["q_k"], qs["q_sgn"],
                qs["q_scl"], qs["q_inv"])
            em("scalar", lambda e: e.activation(q_absf[:, 0:K], x_bf, AF.Abs, bias=0.0, scale=1.0),
               reads=[r_in], writes=[rqs], tag=tag + "abs")
            em("scalar", lambda e: e.activation(q_sgn[:, 0:K], x_bf, AF.Sign, bias=0.0, scale=1.0),
               reads=[r_in, rqs], tag=tag + "sgn")
            av = q_absf[:, 0:K].rearrange("p (n k) -> p n k", k=16)
            em("vector", lambda e: e.tensor_reduce(q_scl[:, 0:nb].unsqueeze(2), av, AX.X, ALU.max,
                                                   apply_absolute_value=True), reads=[rqs], tag=tag + "red")
            em("vector", lambda e: e.tensor_scalar(q_scl[:, 0:nb], q_scl[:, 0:nb], 1.0 / 6.0, 1e-12, ALU.mult, ALU.max),
               reads=[rqs], tag=tag)
            em("vector", lambda e: e.reciprocal(q_inv[:, 0:nb], q_scl[:, 0:nb]), reads=[rqs], tag=tag)
            iv = q_inv[:, 0:nb].unsqueeze(2).broadcast_to([128, nb, 16])
            em("vector", lambda e: e.tensor_tensor(av, av, iv, ALU.mult), reads=[rqs], tag=tag + "u")
            uf = q_absf[:, 0:K]
            em("vector", lambda e: e.tensor_scalar(q_u[:, 0:K], uf, 1.0, None, ALU.mult), reads=[rqs], tag=tag + "ub")
            u = q_u[:, 0:K]
            em("vector", lambda e: e.tensor_scalar(q_h[:, 0:K], u, 0.25, None, ALU.is_gt), reads=[rqs], tag=tag)
            for th in (0.75, 1.25, 1.75):
                em("vector", lambda e, th=th: e.scalar_tensor_tensor(q_h[:, 0:K], u, th, q_h[:, 0:K], ALU.is_gt, ALU.add),
                   reads=[rqs], tag=tag)
            uk = uf if kf32 else u
            em("vector", lambda e, uk=uk: e.tensor_scalar(q_k[:, 0:K], uk, 5.0, None, ALU.is_gt), reads=[rqs], tag=tag)
            for th in (5.0, 3.5, 2.5):
                em("vector", lambda e, th=th, uk=uk: e.scalar_tensor_tensor(q_k[:, 0:K], uk, th, q_k[:, 0:K], ALU.is_gt, ALU.add),
                   reads=[rqs], tag=tag)
            em("vector", lambda e: e.scalar_tensor_tensor(q_h[:, 0:K], q_h[:, 0:K], 0.5, q_k[:, 0:K], ALU.mult, ALU.add),
               reads=[rqs], tag=tag)
            em("vector", lambda e: e.tensor_mul(q_h[:, 0:K], q_h[:, 0:K], q_sgn[:, 0:K]), reads=[rqs], tag=tag)
            sb = q_scl[:, 0:nb].unsqueeze(2).broadcast_to([128, nb, 16])
            t_out = em("vector", lambda e: e.tensor_tensor(
                out_bf.rearrange("p (n k) -> p n k", k=16),
                q_h[:, 0:K].rearrange("p (n k) -> p n k", k=16), sb, ALU.mult),
                reads=[rqs], writes=[r_out], tag=tag + "out")
            return t_out

        def transpose_tiles(srcs, psum_ap, psum_res, dst_view, r_src, r_dst, extra=(), tag="tr"):
            n = len(srcs)
            parts = [(lambda e, i=i, s=srcs[i]: e.matmul(psum_ap[:, i * 128:(i + 1) * 128], s, i128b[:],
                                                          start=True, stop=True)) for i in range(n)]
            mm_t = mmg(parts, reads=[r_src, r_const], writes=[psum_res], extra=extra, tag=tag)
            return em("scalar", lambda e: e.activation(
                dst_view, psum_ap[:, 0:n * 128], AF.Copy, bias=0.0, scale=1.0),
                reads=[psum_res], writes=[r_dst], tag=tag + "ev")

        ph0A = ExitStack()
        e0A = ph0A.enter_context
        s8 = e0A(nc.sbuf_tensor("s8", [8, 1024], F32))
        tribig = e0A(nc.sbuf_tensor("tribig", [128, 128], F32))
        itile = e0A(nc.sbuf_tensor("itile", [128, 1024], F32))
        ones = e0A(nc.sbuf_tensor("ones", [1, 128], F32))
        onesb = e0A(nc.sbuf_tensor("onesb", [128, 1], BF16))
        maskb = e0A(nc.sbuf_tensor("maskb", [128, 32], BF16))
        maska = e0A(nc.sbuf_tensor("maska", [128, 32], BF16))
        wqt = e0A(nc.sbuf_tensor("wqt", [128, 8 * 512], BF16))
        wkt = e0A(nc.sbuf_tensor("wkt", [128, 8 * 512], BF16))
        walt = e0A(nc.sbuf_tensor("walt", [128, 8 * 512], BF16))
        wvt = e0A(nc.sbuf_tensor("wvt", [128, 8 * 1024], BF16))
        wot = e0A(nc.sbuf_tensor("wot", [128, 8 * 1024], BF16))
        CN = {"c_s8": s8, "c_tribig": tribig, "c_itile": itile, "c_ones": ones,
              "c_onesb": onesb, "c_i128b": i128b, "c_maskb": maskb, "c_maska": maska}
        for k in CN:
            em("sync", lambda e, t=CN[k], s=cin[k]: e.dma_start(out=t[:], in_=s if t is not ones else s[0:1, :]),
               sem=s_const, inc=16, tag="ldc")
        em("sync", lambda e: e.dma_start(out=cst[:, 0:5], in_=cin["c_cst"]), sem=s_const, inc=16, tag="ldc")
        t_ld = em("sync", lambda e: e.dma_start(out=walb[:], in_=walb_in),
                  writes=[r_const], sem=s_const, inc=16, tag="ldc")
        with ExitStack() as ph0:
            e0 = ph0.enter_context
            wld = e0(nc.sbuf_tensor("wld", [128, 8192], F32))
            wldb = e0(nc.sbuf_tensor("wldb", [128, 8192], BF16))
            r_wld, r_wldb = Res("wld"), Res("wldb")

            def prep_attn_w(src, rows, dstT, r_dst, ncols, tag):
                nrt = rows // 128
                em("sync", lambda e: e.dma_start(
                    out=wld[:, 0:nrt * 1024].rearrange("p (rt k) -> p rt k", k=1024),
                    in_=src.rearrange("(rt p) k -> p rt k", p=128)),
                   writes=[r_wld], sem=s_w, inc=16, tag=tag + "ld")
                em("scalar", lambda e: e.activation(wldb[:, 0:nrt * 1024], wld[:, 0:nrt * 1024],
                                                    AF.Copy, bias=0.0, scale=1.0),
                   reads=[r_wld], writes=[r_wldb], extra=[t_ld], tag=tag + "cast")
                for rt in range(nrt):
                    for g2 in range(2):
                        srcs = [wldb[:, rt * 1024 + (g2 * 4 + q2) * 128: rt * 1024 + (g2 * 4 + q2 + 1) * 128]
                                for q2 in range(4)]
                        dv = dstT[:].rearrange("p (kc n) -> p kc n", n=ncols)[:, g2 * 4:(g2 + 1) * 4,
                                                                             rt * 128:(rt + 1) * 128]
                        pi = (rt * 2 + g2) % 2
                        transpose_tiles(srcs, pC[:, pi * 512:(pi + 1) * 512], rC[pi], dv,
                                        r_wldb, r_dst, tag=tag + "tr")

            prep_attn_w(wq_in, 512, wqt, r_wqt, 512, "wq")
            prep_attn_w(wk_in, 512, wkt, r_wkt, 512, "wk")
            prep_attn_w(wal_in, 512, walt, r_walt, 512, "wal")
            prep_attn_w(wv_in, 1024, wvt, r_wvt, 1024, "wv")
            prep_attn_w(wo_in, 1024, wot, r_wot, 1024, "wo")

        barW = engine_barrier("barW")
        with ExitStack() as ph0f:
            e0 = ph0f.enter_context
            wldF = [e0(nc.sbuf_tensor(f"wldF{i}", [128, I], F32)) for i in range(2)]
            r_wldF = [Res("wldF0"), Res("wldF1")]
            s_wf = [eg(nc.semaphore(f"s_wf{i}")) for i in range(2)]
            wqtm2 = [e0(nc.sbuf_tensor(f"wqtm{i}", [128, I], BF16)) for i in range(2)]
            wqfm2 = [e0(nc.sbuf_tensor(f"wqfm{i}", [128, I], BF16)) for i in range(2)]
            r_wqtm2 = [Res("wqtm0"), Res("wqtm1")]
            r_wqfm2 = [Res("wqfm0"), Res("wqfm1")]
            qs02 = [alloc_qscr(e0, I, f"p0q{i}", alias_k=False) for i in range(2)]
            rqs02 = [Res("qs0a"), Res("qs0b")]

            def prep_ffn_w(src, rows, K, dst_d, rti0, tag, kf32=True):
                nrt = rows // 128
                nkc = K // 128
                for rt in range(nrt):
                    pi = (rti0 + rt) % 2
                    em("sync", lambda e, rt=rt, pi=pi: e.dma_start(out=wldF[pi][:, 0:K], in_=src[rt * 128:(rt + 1) * 128, :]),
                       writes=[r_wldF[pi]], extra=barW, sem=s_wf[pi], inc=16, tag=tag + "ld")
                    quant(wldF[pi][:, 0:K], K, wqtm2[pi][:, 0:K], r_wldF[pi], r_wqtm2[pi],
                          qs02[pi], rqs02[pi], kf32=kf32, tag=tag + "q")
                    for gq in range(0, nkc, 4):
                        ng = min(4, nkc - gq)
                        srcs = [wqtm2[pi][:, (gq + q2) * 128:(gq + q2 + 1) * 128] for q2 in range(ng)]
                        dv = wqfm2[pi][:, gq * 128:(gq + ng) * 128]
                        transpose_tiles(srcs, pC[:, (pi) * 512:(pi + 1) * 512], rC[pi], dv,
                                        r_wqtm2[pi], r_wqfm2[pi], tag=tag + "tr")
                    em("sync", lambda e, rt=rt, pi=pi: e.dma_start(
                        out=dst_d.rearrange("(kc p) r -> p kc r", p=128)[:, :, rt * 128:(rt + 1) * 128],
                        in_=wqfm2[pi][:, 0:nkc * 128].rearrange("p (kc r) -> p kc r", r=128)),
                        reads=[r_wqfm2[pi]], writes=[r_wqd_scr], sem=s_scr, inc=16, tag=tag + "st")
                return rti0 + nrt

            rti = prep_ffn_w(wg_in, I, 1024, wqgt_d, 0, "wg", kf32=False)
            rti = prep_ffn_w(wu_in, I, 1024, wqut_d, rti, "wu", kf32=False)
            rti = prep_ffn_w(wd_in, 1024, I, wqdt_d, rti, "wd", kf32=True)

        bar0 = engine_barrier("bar0")
        bar0.append(r_wqd_scr.writer)
        with ExitStack() as phA:
            eA = phA.enter_context
            xt = [eA(nc.sbuf_tensor(f"xt{i}", [128, 1024], F32)) for i in range(2)]
            xrt = [eA(nc.sbuf_tensor(f"xrt{i}", [128, 1024], F32)) for i in range(2)]
            r_xt = [Res("xt0"), Res("xt1")]
            r_xrt = [Res("xrt0"), Res("xrt1")]
            qsA = alloc_qscr(eA, 1024, "pA")
            junk = qsA["q_absf"]
            lnt = eA(nc.sbuf_tensor("lnt", [128, 8], F32))
            r_lnt = Res("lnt")
            xh_tm = eA(nc.sbuf_tensor("xh_tm", [128, 4 * 1024], BF16))
            xh_fm = eA(nc.sbuf_tensor("xh_fm", [128, 8 * 512], BF16))
            q_fm = eA(nc.sbuf_tensor("q_fm", [128, 4 * 512], BF16))
            qk_fm = eA(nc.sbuf_tensor("qk_fm", [128, 4 * 512], BF16))
            sp_fm = eA(nc.sbuf_tensor("sp_fm", [128, 4 * 512], BF16))
            al_e = eA(nc.sbuf_tensor("al_e", [128, 512], BF16))
            al_l = eA(nc.sbuf_tensor("al_l", [128, 512], BF16))
            al_r = eA(nc.sbuf_tensor("al_r", [128, 512], BF16))
            sgn_v = eA(nc.sbuf_tensor("sgn_v", [128, 512], BF16))
            av_v = qsA["q_absf"]
            vt_t = eA(nc.sbuf_tensor("vt_t", [128, 4 * 1024], BF16))
            a_hm = eA(nc.sbuf_tensor("a_hm", [8, TT], F32))
            b_hm = eA(nc.sbuf_tensor("b_hm", [8, TT], F32))
            y_tm = eA(nc.sbuf_tensor("y_tm", [128, 1024], BF16))
            y_fm = eA(nc.sbuf_tensor("y_fm", [128, 4 * 1024], BF16))
            x2sb = eA(nc.sbuf_tensor("x2sb", [128, 1024], F32))
            h2sb = eA(nc.sbuf_tensor("h2sb", [128, 1024], F32))
            h2q2 = [eA(nc.sbuf_tensor(f"h2q{i}", [128, 1024], BF16)) for i in range(2)]
            h2qf = eA(nc.sbuf_tensor("h2qf", [128, 1024], BF16))
            (r_xhtm, r_xhfm, r_qfm, r_qkfm, r_spfm, r_ale, r_all, r_alr, r_sgnv,
             r_vt, r_ahm, r_bhm, r_ytm, r_yfm, r_x2, r_h2,
             r_h2qf) = (Res(x) for x in
                        ("xh_tm", "xh_fm", "q_fm", "qk_fm", "sp_fm", "al_e", "al_l", "al_r",
                         "sgn_v", "vt_t", "a_hm", "b_hm", "y_tm", "y_fm", "x2",
                         "h2", "h2qf"))
            r_h2q2 = [Res("h2q0"), Res("h2q1")]
            lam = eA(nc.sbuf_tensor("lam", [8, 1], F32))
            Rt = eA(nc.sbuf_tensor("Rt", [8, D], F32))
            Rb = eA(nc.sbuf_tensor("Rb", [8, D], BF16))
            r_cs = eA(nc.sbuf_tensor("r_cs", [8, C], F32))
            w_hm = eA(nc.sbuf_tensor("w_hm", [8, C], F32))
            M_hm = eA(nc.sbuf_tensor("M_hm", [8, C], F32))
            negM = eA(nc.sbuf_tensor("negM", [8, C], F32))
            negM_cat = eA(nc.sbuf_tensor("negM_cat", [1, 1024], F32))
            Emm = eA(nc.sbuf_tensor("Emm", [128, 1024], BF16))
            Ett = qsA["q_u"]
            zc_row = eA(nc.sbuf_tensor("zc_row", [1, 1024], F32))
            zrow = eA(nc.sbuf_tensor("zrow", [1, 1024], F32))
            ciz = eA(nc.sbuf_tensor("ciz", [1, 2048], F32))
            c_row = ciz[0:1, 0:1024]
            izrow = ciz[0:1, 1024:2048]
            cizhm = eA(nc.sbuf_tensor("cizhm", [8, 256], F32))
            c_hm = cizhm[:, 0:128]
            iz_hm = cizhm[:, 128:256]
            c_hmb = eA(nc.sbuf_tensor("c_hmb", [8, C], BF16))
            wcol = eA(nc.sbuf_tensor("wcol", [128, 512], BF16))
            wtot = eA(nc.sbuf_tensor("wtot", [128, 128], BF16))
            lnzt = eA(nc.sbuf_tensor("lnzt", [8, 1], F32))
            (r_lam, r_R, r_Rb, r_rcs, r_whm, r_Mhm, r_negM, r_cat, r_Emm, r_Et,
             r_zc, r_z, r_iz, r_cr, r_chm, r_chmb, r_izhm, r_wcol, r_wtot,
             r_lnz, r_tsc) = (Res(x) for x in
                              ("lam", "R", "Rb", "rcs", "whm", "Mhm", "negM", "cat", "Emm",
                               "Et", "zc", "z", "iz", "cr", "chm", "chmb", "izhm", "wcol",
                               "wtot", "lnz", "tsc"))

            SX, SSQ, MEAN, VAR, RSTD, NMR = (lnt[:, i:i + 1] for i in range(6))

            def ln_emit(src_ap, r_src, dst_ap, r_dst, tag):
                em("vector", lambda e: e.tensor_scalar(junk[:], src_ap, 0.0, 0.0, ALU.add, ALU.add,
                                                       accum_out=SX), reads=[r_src], writes=[r_qs, r_lnt], tag=tag)
                em("vector", lambda e: e.scalar_tensor_tensor(junk[:], src_ap, 1.0, src_ap, ALU.bypass, ALU.mult,
                                                              accum_out=SSQ), reads=[r_src], writes=[r_qs], tag=tag)
                em("vector", lambda e: e.tensor_scalar(MEAN, SX, 1.0 / D, None, ALU.mult), tag=tag)
                em("vector", lambda e: e.tensor_mul(NMR, MEAN, MEAN), tag=tag)
                em("vector", lambda e: e.scalar_tensor_tensor(VAR, SSQ, 1.0 / D, NMR, ALU.mult, ALU.subtract),
                   writes=[r_lnt], tag=tag)
                em("scalar", lambda e: e.activation(VAR, VAR, AF.Ln, bias=c_lneps, scale=1.0),
                   reads=[r_lnt], extra=[t_ld], tag=tag)
                em("scalar", lambda e: e.activation(RSTD, VAR, AF.Exp, bias=0.0, scale=c_neghalf),
                   writes=[r_lnt], tag=tag)
                em("vector", lambda e: e.scalar_tensor_tensor(NMR, MEAN, -1.0, RSTD, ALU.mult, ALU.mult),
                   reads=[r_lnt], writes=[r_lnt], tag=tag)
                return em("scalar", lambda e: e.activation(dst_ap, src_ap, AF.Identity, bias=NMR, scale=RSTD),
                          reads=[r_src, r_lnt], writes=[r_dst], tag=tag + "n")

            pendA = None

            def h2_tail(b, t0, sub):
                h2q = h2q2[sub % 2]
                for g2 in range(2):
                    srcs = [h2q[:, (g2 * 4 + q2) * 128:(g2 * 4 + q2 + 1) * 128] for q2 in range(4)]
                    transpose_tiles(srcs, pC[:, g2 * 512:(g2 + 1) * 512], rC[g2],
                                    h2qf[:, g2 * 512:(g2 + 1) * 512],
                                    r_h2q2[sub % 2], r_h2qf, tag="h2T")
                em("sync", lambda e, b=b, t0=t0, sub=sub: e.dma_start(
                    out=h2qt_d.rearrange("(kc p) t -> p kc t", p=128)[:, :, b * S + t0 + sub * 128: b * S + t0 + (sub + 1) * 128],
                    in_=h2qf[:].rearrange("p (kc r) -> p kc r", r=128)),
                    reads=[r_h2qf], writes=[r_h2d], sem=s_sh2, inc=16, tag="h2st")

            em("vector", lambda e: e.memset(lam[:], -1e30), writes=[r_lam], extra=[t_ld], tag="init")
            em("vector", lambda e: e.memset(Rt[:], 0.0), writes=[r_R], tag="init")
            em("vector", lambda e: e.memset(Rb[:], 0.0), writes=[r_Rb], tag="init")

            for b in range(Bc if stage >= 1 else 0):
                if b > 0:
                    em("vector", lambda e: e.memset(lam[:], -1e30), writes=[r_lam], tag="reinit")
                    em("vector", lambda e: e.memset(Rt[:], 0.0), writes=[r_R], tag="reinit")
                    em("vector", lambda e: e.memset(Rb[:], 0.0), writes=[r_Rb], tag="reinit")
                for T in range(NT):
                    t0 = T * TT
                    for sub in range(NSUB if "a" in sections else 0):
                        ri = sub % 2
                        em("sync", lambda e, b=b, t0=t0, sub=sub, ri=ri: e.dma_start(
                            out=xt[ri][:], in_=x_in[b, t0 + sub * 128:t0 + (sub + 1) * 128, :]),
                            writes=[r_xt[ri]], extra=bar0, sem=s_x[ri], inc=16, tag="xld")
                        ln_emit(xt[ri][:], r_xt[ri],
                                xh_tm[:, sub * 1024:(sub + 1) * 1024], r_xhtm, f"ln1")
                    for sub in range(NSUB if "b" in sections else 0):
                        for g2 in range(2):
                            srcs = [xh_tm[:, sub * 1024 + (g2 * 4 + q2) * 128: sub * 1024 + (g2 * 4 + q2 + 1) * 128]
                                    for q2 in range(4)]
                            dv = xh_fm[:].rearrange("p (kc t) -> p kc t", t=512)[:, g2 * 4:(g2 + 1) * 4,
                                                                                sub * 128:(sub + 1) * 128]
                            transpose_tiles(srcs, pB[:, g2 * 512:(g2 + 1) * 512], rB[g2], dv,
                                            r_xhtm, r_xhfm, tag="xhT")
                    for ncnk in range(4 if "c" in sections else 0):
                        hf = ncnk % 2
                        psA = pA[:, hf * 512:(hf + 1) * 512]
                        mmg([(lambda e, kc=kc, ncnk=ncnk, psA=psA: e.matmul(
                            psA, wqt[:, kc * 512 + ncnk * 128: kc * 512 + (ncnk + 1) * 128],
                            xh_fm[:, kc * 512:(kc + 1) * 512], start=(kc == 0), stop=(kc == 7)))
                            for kc in range(8)], reads=[r_wqt, r_xhfm], writes=[rA[hf]], tag="qmm")
                        em("scalar", lambda e, ncnk=ncnk, psA=psA: e.activation(
                            q_fm[:, ncnk * 512:(ncnk + 1) * 512], psA, AF.Copy, bias=0.0, scale=1.0),
                            reads=[rA[hf]], writes=[r_qfm], tag="qev")
                        mmg([(lambda e, kc=kc, ncnk=ncnk, psA=psA: e.matmul(
                            psA, wkt[:, kc * 512 + ncnk * 128: kc * 512 + (ncnk + 1) * 128],
                            xh_fm[:, kc * 512:(kc + 1) * 512], start=(kc == 0), stop=(kc == 7)))
                            for kc in range(8)], reads=[r_wkt, r_xhfm], writes=[rA[hf]], tag="kmm")
                        em("vector", lambda e, ncnk=ncnk, psA=psA: e.tensor_tensor(
                            qk_fm[:, ncnk * 512:(ncnk + 1) * 512], q_fm[:, ncnk * 512:(ncnk + 1) * 512],
                            psA, ALU.mult), reads=[rA[hf], r_qfm], writes=[r_qkfm], tag="qk")
                        if "g" not in sections:
                            continue
                        psB = pB[:, hf * 512:(hf + 1) * 512]
                        mmg([(lambda e, kc=kc, ncnk=ncnk, psB=psB: e.matmul(
                            psB, walt[:, kc * 512 + ncnk * 128: kc * 512 + (ncnk + 1) * 128],
                            xh_fm[:, kc * 512:(kc + 1) * 512], start=(kc == 0), stop=(kc == 7)))
                            for kc in range(8)], reads=[r_walt, r_xhfm], writes=[rB[hf]], tag="almm")
                        balc = walb[:, ncnk:ncnk + 1]
                        if "1" in sections:
                            em("scalar", lambda e, psB=psB, balc=balc: e.activation(al_e[:], psB, AF.Abs, bias=balc, scale=1.0),
                               reads=[rB[hf], r_const], writes=[r_ale], tag="spabs")
                        if "2" in sections:
                            em("scalar", lambda e: e.activation(al_e[:], al_e[:], AF.Exp, bias=0.0, scale=c_negone),
                               reads=[r_ale], writes=[r_ale], tag="spexp")
                        if "3" in sections:
                            em("scalar", lambda e: e.activation(al_l[:], al_e[:], AF.Ln, bias=1.0, scale=1.0),
                               reads=[r_ale], writes=[r_all], tag="spln")
                        if "4" in sections:
                            em("scalar", lambda e, psB=psB, balc=balc: e.activation(al_r[:], psB, AF.Relu, bias=balc, scale=1.0),
                               reads=[rB[hf], r_const], writes=[r_alr], tag="sprelu")
                        if "5" in sections:
                            em("vector", lambda e, ncnk=ncnk: e.tensor_add(
                                sp_fm[:, ncnk * 512:(ncnk + 1) * 512], al_l[:], al_r[:]),
                                reads=[r_all, r_alr], writes=[r_spfm], tag="sp")
                    if "h" not in sections:
                        continue
                    mmg([(lambda e, kc=kc: e.matmul(pN[0:8, 0:512], maskb[:, kc * 8:(kc + 1) * 8],
                                                    qk_fm[:, kc * 512:(kc + 1) * 512],
                                                    start=(kc == 0), stop=(kc == 3))) for kc in range(4)],
                        reads=[r_const, r_qkfm], writes=[rN[0]], tag="bmm")
                    mmg([(lambda e, kc=kc: e.matmul(pN[0:8, 512:1024], maska[:, kc * 8:(kc + 1) * 8],
                                                    sp_fm[:, kc * 512:(kc + 1) * 512],
                                                    start=(kc == 0), stop=(kc == 3))) for kc in range(4)],
                        reads=[r_const, r_spfm], writes=[rN[1]], tag="amm")
                    em("vector", lambda e: e.tensor_copy(b_hm[:], pN[0:8, 0:512]),
                       reads=[rN[0]], writes=[r_bhm], tag="bev")
                    em("vector", lambda e: e.tensor_copy(a_hm[:], pN[0:8, 512:1024]),
                       reads=[rN[1]], writes=[r_ahm], tag="aev")
                    if T == 0:
                        em("vector", lambda e: e.memset(a_hm[:, 0:1], 0.0), writes=[r_ahm], tag="a0")
                    for sub in range(NSUB if "d" in sections else 0):
                        for hf in range(2):
                            psC = pC[:, hf * 512:(hf + 1) * 512]
                            mmg([(lambda e, kc=kc, sub=sub, hf=hf, psC=psC: e.matmul(
                                psC, xh_fm[:, kc * 512 + sub * 128: kc * 512 + (sub + 1) * 128],
                                wvt[:, kc * 1024 + hf * 512: kc * 1024 + (hf + 1) * 512],
                                start=(kc == 0), stop=(kc == 7))) for kc in range(8)],
                                reads=[r_xhfm, r_wvt], writes=[rC[hf]], tag="vmm")
                            em("scalar", lambda e, psC=psC: e.activation(sgn_v[:], psC, AF.Sign, bias=0.0, scale=1.0),
                               reads=[rC[hf]], writes=[r_sgnv], tag="vsgn")
                            em("scalar", lambda e, psC=psC: e.activation(av_v[:, 0:512], psC, AF.Abs, bias=0.0, scale=1.0),
                               reads=[rC[hf]], writes=[r_qs], tag="vabs")
                            em("vector", lambda e: e.tensor_scalar(av_v[:, 0:512], av_v[:, 0:512], 1.0 + EPS, EPS * (1.0 + EPS),
                                                                   ALU.mult, ALU.add),
                               reads=[r_qs], writes=[r_qs], tag="veps")
                            if T == 0 and sub == 0:
                                em("vector", lambda e: e.tensor_scalar(av_v[0:1, 0:512], av_v[0:1, 0:512],
                                                                       1.0 / (1.0 + EPS), 0.0, ALU.mult, ALU.add),
                                   reads=[r_qs], writes=[r_qs], tag="veps0")
                            em("vector", lambda e, sub=sub, hf=hf: e.tensor_mul(
                                vt_t[:, sub * 1024 + hf * 512: sub * 1024 + (hf + 1) * 512], av_v[:, 0:512], sgn_v[:]),
                                reads=[r_qs, r_sgnv], writes=[r_vt], tag="vt")
                    for jj in range(CPT if "e" in sections else 0):
                        sl = slice(jj * C, (jj + 1) * C)
                        em("vector", lambda e, sl=sl: e.tensor_tensor_scan(r_cs[:], a_hm[:, sl], a_hm[:, sl],
                                                                           0.0, ALU.add, ALU.bypass),
                           reads=[r_ahm], writes=[r_rcs], tag="r")
                        em("vector", lambda e, sl=sl: e.tensor_sub(w_hm[:], b_hm[:, sl], r_cs[:]),
                           reads=[r_bhm, r_rcs], writes=[r_whm], tag="w")
                        em("vector", lambda e: e.tensor_tensor_scan(M_hm[:], w_hm[:], w_hm[:], -1e30, ALU.max, ALU.max),
                           reads=[r_whm], writes=[r_Mhm], tag="M")
                        em("vector", lambda e: e.tensor_scalar(negM[:], M_hm[:], -1.0, None, ALU.mult),
                           reads=[r_Mhm], writes=[r_negM], tag="negm")
                        em("scalar", lambda e: e.dma_start(
                            out=negM_cat[0:1, :].rearrange("p (hh c) -> p hh c", hh=8), in_=negM[:]),
                            reads=[r_negM], writes=[r_cat], sem=s_cat, inc=16, tag="cat")
                        for hf in range(2):
                            hs = slice(hf * 512, (hf + 1) * 512)
                            mmg([
                                (lambda e, hs=hs: e.matmul(pA[:, hs], w_hm[:], s8[:, hs], start=True, stop=False)),
                                (lambda e, hs=hs: e.matmul(pA[:, hs], ones[0:1, :], negM_cat[0:1, hs], start=False, stop=False)),
                                (lambda e, hs=hs: e.matmul(pA[:, hs], tribig[:], itile[:, hs], start=False, stop=True)),
                            ], reads=[r_whm, r_cat, r_const], writes=[rA[hf]], tag="Dmm")
                        em("scalar", lambda e: e.activation(Emm[:], pA[:], AF.Exp, bias=0.0, scale=1.0),
                           reads=[rA[0], rA[1]], writes=[r_Emm], tag="E")
                        for hf in range(2):
                            hs = slice(hf * 512, (hf + 1) * 512)
                            mmg([
                                (lambda e, hs=hs: e.matmul(pB[0:1, hs], lam[:], s8[:, hs], start=True, stop=False)),
                                (lambda e, hs=hs: e.matmul(pB[0:1, hs], ones[0:1, 0:1], negM_cat[0:1, hs], start=False, stop=True)),
                            ], reads=[r_lam, r_cat, r_const], writes=[rB[hf]], tag="zmm")
                        em("scalar", lambda e: e.activation(zc_row[:], pB[0:1, :], AF.Exp, bias=0.0, scale=1.0),
                           reads=[rB[0], rB[1]], writes=[r_zc], tag="zc")
                        for hf in range(2):
                            hs = slice(hf * 512, (hf + 1) * 512)
                            mmg([(lambda e, hs=hs: e.matmul(pB[0:1, hs], onesb[:], Emm[:, hs], start=True, stop=True))],
                                reads=[r_Emm, r_const, r_zc], writes=[rB[hf]], tag="cs")
                        em("vector", lambda e: e.tensor_add(zrow[:], zc_row[:], pB[0:1, :]),
                           reads=[r_zc, rB[0], rB[1]], writes=[r_z], tag="z")
                        em("scalar", lambda e: e.activation(zrow[:], zrow[:], AF.Ln, bias=0.0, scale=1.0),
                           reads=[r_z], writes=[r_z], tag="izln")
                        em("scalar", lambda e: e.activation(izrow, zrow[:], AF.Exp, bias=0.0, scale=cst[0:1, 2:3]),
                           reads=[r_z, r_const], writes=[r_iz], tag="izr")
                        em("vector", lambda e: e.tensor_mul(c_row, zc_row[:], izrow),
                           reads=[r_zc, r_iz], writes=[r_cr], tag="cr")
                        em("scalar", lambda e: e.dma_start(
                            out=cizhm[:, 0:128], in_=ciz[0:1, 0:1024].rearrange("p (hh c) -> p hh c", hh=8)),
                            reads=[r_cr], writes=[r_chm], sem=s_c, inc=16, tag="chm")
                        em("scalar", lambda e: e.dma_start(
                            out=cizhm[:, 128:256], in_=ciz[0:1, 1024:2048].rearrange("p (hh c) -> p hh c", hh=8)),
                            reads=[r_iz], writes=[r_izhm], sem=s_iz, inc=16, tag="izhm")
                        em("vector", lambda e: e.tensor_scalar(c_hmb[:], c_hm, 1.0, None, ALU.mult),
                           reads=[r_chm], writes=[r_chmb], tag="chb")
                        for hf in range(2):
                            hs = slice(hf * 512, (hf + 1) * 512)
                            mmg([(lambda e, hs=hs: e.matmul(pB[:, hs], ones[0:1, :], izrow[0:1, hs], start=True, stop=True))],
                                reads=[r_iz, r_const], writes=[rB[hf]], tag="ib")
                        em("vector", lambda e: e.tensor_mul(Ett[:], Emm[:], pB[:]),
                           reads=[r_Emm, rB[0], rB[1]], writes=[r_qs], tag="Et")
                        em("vector", lambda e: e.tensor_add(wcol[:], Ett[:, 0:512], Ett[:, 512:1024]),
                           reads=[r_qs], writes=[r_wcol], tag="c1")
                        em("vector", lambda e: e.tensor_add(wcol[:, 0:256], wcol[:, 0:256], wcol[:, 256:512]),
                           reads=[r_wcol], writes=[r_wcol], tag="c2")
                        em("vector", lambda e: e.tensor_add(wtot[:], wcol[:, 0:128], wcol[:, 128:256]),
                           reads=[r_wcol], writes=[r_wtot], tag="wtot")
                        for hf in range(2):
                            hs = slice(hf * 512, (hf + 1) * 512)
                            vsl = slice(jj * 1024 + hf * 512, jj * 1024 + (hf + 1) * 512)
                            mmg([
                                (lambda e, hs=hs, vsl=vsl: e.matmul(pA[:, hs], wtot[:], vt_t[:, vsl], start=True, stop=False)),
                                (lambda e, hs=hs: e.matmul(pA[:, hs], c_hmb[:], Rb[:, hs], start=False, stop=True)),
                            ], reads=[r_wtot, r_vt, r_chmb, r_Rb], writes=[rA[hf]], tag="Ymm")
                        Esl = Emm[:].rearrange("p (hh c) -> p hh c", hh=8)[:, :, 127:128].squeeze(2)
                        for hf in range(2):
                            hs = slice(hf * 512, (hf + 1) * 512)
                            vsl = slice(jj * 1024 + hf * 512, jj * 1024 + (hf + 1) * 512)
                            mmg([(lambda e, hs=hs, vsl=vsl: e.matmul(pN[0:8, hs], Esl, vt_t[:, vsl], start=True, stop=True))],
                                reads=[r_Emm, r_vt], writes=[rN[hf]], tag="NE")
                        em("scalar", lambda e: e.activation(y_tm[:], pA[:], AF.Copy, bias=0.0, scale=1.0),
                           reads=[rA[0], rA[1]], writes=[r_ytm], tag="yev")
                        for g2 in range(2):
                            srcs = [y_tm[:, (g2 * 4 + q2) * 128:(g2 * 4 + q2 + 1) * 128] for q2 in range(4)]
                            dv = y_fm[:].rearrange("p (kc t) -> p kc t", t=512)[:, g2 * 4:(g2 + 1) * 4,
                                                                               jj * 128:(jj + 1) * 128]
                            transpose_tiles(srcs, pC[:, g2 * 512:(g2 + 1) * 512], rC[g2], dv,
                                            r_ytm, r_yfm, tag="yT")
                        em("vector", lambda e: e.tensor_scalar(Rt[:], Rt[:], cizhm[:, 127:128], None, ALU.mult),
                           reads=[r_R, r_chm], writes=[r_R], tag="Rdec")
                        em("vector", lambda e: e.scalar_tensor_tensor(Rt[:], pN[0:8, :], cizhm[:, 255:256], Rt[:],
                                                                      ALU.mult, ALU.add),
                           reads=[rN[0], rN[1], r_izhm, r_R], writes=[r_R], tag="R")
                        em("vector", lambda e: e.tensor_scalar(Rb[:], Rt[:], 1.0, None, ALU.mult),
                           reads=[r_R], writes=[r_Rb], tag="Rbc")
                        em("scalar", lambda e: e.activation(lnzt[:], cizhm[:, 255:256], AF.Ln, bias=0.0, scale=1.0),
                           reads=[r_izhm], writes=[r_lnz], tag="lnu")
                        em("vector", lambda e: e.tensor_sub(lam[:], M_hm[:, 127:128], lnzt[:]),
                           reads=[r_Mhm, r_lnz], writes=[r_lam], tag="lam1")
                        em("vector", lambda e: e.tensor_add(lam[:], lam[:], r_cs[:, 127:128]),
                           reads=[r_rcs, r_lam], writes=[r_lam], tag="lam2")
                    for sub in range(NSUB if "f" in sections else 0):
                        ri = sub % 2
                        em("sync", lambda e, b=b, t0=t0, sub=sub, ri=ri: e.dma_start(
                            out=xrt[ri][:], in_=x_in[b, t0 + sub * 128:t0 + (sub + 1) * 128, :]),
                            writes=[r_xrt[ri]], extra=bar0, sem=s_xr[ri], inc=16, tag="xrld")
                        for hf in range(2):
                            psC = pC[:, hf * 512:(hf + 1) * 512]
                            mmg([(lambda e, kc=kc, sub=sub, hf=hf, psC=psC: e.matmul(
                                psC, y_fm[:, kc * 512 + sub * 128: kc * 512 + (sub + 1) * 128],
                                wot[:, kc * 1024 + hf * 512: kc * 1024 + (hf + 1) * 512],
                                start=(kc == 0), stop=(kc == 7))) for kc in range(8)],
                                reads=[r_yfm, r_wot], writes=[rC[hf]], tag="omm")
                            em("vector", lambda e, hf=hf, psC=psC, ri=ri: e.tensor_add(
                                x2sb[:, hf * 512:(hf + 1) * 512], psC, xrt[ri][:, hf * 512:(hf + 1) * 512]),
                                reads=[rC[hf], r_xrt[ri]], writes=[r_x2] if hf == 1 else [], tag="x2")
                        em("sync", lambda e, b=b, t0=t0, sub=sub: e.dma_start(
                            out=x2_d[b * S + t0 + sub * 128: b * S + t0 + (sub + 1) * 128, :], in_=x2sb[:]),
                            reads=[r_x2], writes=[r_x2d], sem=s_sx2, inc=16, tag="x2st")
                        ln_emit(x2sb[:], r_x2, h2sb[:], r_h2, "ln2")
                        quant(h2sb[:], 1024, h2q2[sub % 2][:], r_h2, r_h2q2[sub % 2], qsA, tag="h2q")
                        if pendA is not None:
                            h2_tail(*pendA)
                        pendA = (b, t0, sub)

            if pendA is not None:
                h2_tail(*pendA)
        barA = engine_barrier("barA")
        barA += [r_x2d.writer, r_h2d.writer]
        ph0A.close()
        if stage < 2:
            n_tail_sem, n_tail = (s_scr, P.scnt.get(id(s_scr), 0)) if stage == 0 else (s_sh2, P.scnt.get(id(s_sh2), 0))
            block = eg(nc.Block())
            P.finalize(block, tail={"sync": lambda eng: eng.wait_ge(n_tail_sem, n_tail)})
            return nc
        with ExitStack() as phB:
            eB = phB.enter_context
            wqg = eB(nc.sbuf_tensor("wqg", [128, 8 * I], BF16))
            wqu = eB(nc.sbuf_tensor("wqu", [128, 8 * I], BF16))
            wqd = eB(nc.sbuf_tensor("wqd", [128, 22 * 1024], BF16))
            x2t = [eB(nc.sbuf_tensor(f"x2t{i}", [128, 1024], F32)) for i in range(2)]
            h2f = [eB(nc.sbuf_tensor("h2f0", [128, 8 * 128], BF16))] * 2
            hm_tm = eB(nc.sbuf_tensor("hm_tm", [128, I], F32))
            hmq2 = [eB(nc.sbuf_tensor(f"hmq{i}", [128, I], BF16)) for i in range(2)]
            qsB = alloc_qscr(eB, I, "pB")
            hmqf = eB(nc.sbuf_tensor("hmqf", [128, 22 * 128], BF16))
            r_wqg, r_wqu, r_wqd = Res("wqg"), Res("wqu"), Res("wqd")
            r_x2t = [Res("x2t0"), Res("x2t1")]
            _rh = Res("h2f0")
            r_h2f = [_rh, _rh]
            r_hm = Res("hm")
            r_hmq2 = [Res("hmq0"), Res("hmq1")]
            r_hmqf = Res("hmqf")

            em("sync", lambda e: e.dma_start(
                out=wqg[:].rearrange("p (kc r) -> p kc r", r=I),
                in_=wqgt_d.rearrange("(kc p) r -> p kc r", p=128)),
                reads=[r_wqd_scr], writes=[r_wqg], extra=barA, sem=s_w, inc=16, tag="wgld")
            em("sync", lambda e: e.dma_start(
                out=wqu[:].rearrange("p (kc r) -> p kc r", r=I),
                in_=wqut_d.rearrange("(kc p) r -> p kc r", p=128)),
                reads=[r_wqd_scr], writes=[r_wqu], extra=barA, sem=s_w, inc=16, tag="wuld")
            t_wd = em("sync", lambda e: e.dma_start(
                out=wqd[:].rearrange("p (kc r) -> p kc r", r=1024),
                in_=wqdt_d.rearrange("(kc p) r -> p kc r", p=128)),
                reads=[r_wqd_scr], writes=[r_wqd], extra=barA, sem=s_w, inc=16, tag="wdld")
            r_wqg.writer = t_wd
            r_wqu.writer = t_wd

            NSl = [(i * 512, min(I, (i + 1) * 512)) for i in range((I + 511) // 512)]

            def sub_head(b, t0, sub, rx):
                hmq = hmq2[sub % 2]
                for gq in range(0, 22, 4):
                    ng = min(4, 22 - gq)
                    srcs = [hmq[:, (gq + q2) * 128:(gq + q2 + 1) * 128] for q2 in range(ng)]
                    transpose_tiles(srcs, pC[:, 0:512], rC[0],
                                    hmqf[:, gq * 128:(gq + ng) * 128],
                                    r_hmq2[sub % 2], r_hmqf, tag="hmT")

            def sub_tail(b, t0, sub, rx):
                em("sync", lambda e, b=b, t0=t0, sub=sub, rx=rx: e.dma_start(
                    out=x2t[rx][:], in_=x2_d[b * S + t0 + sub * 128: b * S + t0 + (sub + 1) * 128, :]),
                    reads=[r_x2d], writes=[r_x2t[rx]], extra=barA, sem=s_x[rx], inc=16, tag="x2ld")
                outsb = x2t[rx]
                for hf in range(2):
                    psd = pC[:, 512:1024] if hf == 0 else pN[:, 0:512]
                    rsd = rC[1] if hf == 0 else rN[0]
                    mmg([(lambda e, kc=kc, hf=hf, psd=psd: e.matmul(
                        psd, hmqf[:, kc * 128:(kc + 1) * 128],
                        wqd[:, kc * 1024 + hf * 512: kc * 1024 + (hf + 1) * 512],
                        start=(kc == 0), stop=(kc == 21))) for kc in range(22)],
                        reads=[r_hmqf, r_wqd], writes=[rsd], tag="dmm")
                    em("vector", lambda e, hf=hf, rx=rx, psd=psd: e.tensor_add(
                        outsb[:, hf * 512:(hf + 1) * 512], psd,
                        x2t[rx][:, hf * 512:(hf + 1) * 512]),
                        reads=[rsd], writes=[r_x2t[rx]] if hf == 1 else [], tag="oadd")
                em("sync", lambda e, b=b, t0=t0, sub=sub, rx=rx: e.dma_start(
                    out=out_d[b, t0 + sub * 128: t0 + (sub + 1) * 128, :], in_=x2t[rx][:]),
                    reads=[r_x2t[rx]], sem=s_out, inc=16, tag="outdma")

            pend = None
            pend2 = None
            for b in range(Bc):
                for T in range(NT):
                    t0 = T * TT
                    ti = b * NT + T
                    for sub in range(NSUB):
                        rx = sub % 2
                        ri = sub % 2
                        em("sync", lambda e, b=b, t0=t0, sub=sub, ri=ri: e.dma_start(
                            out=h2f[ri][:].rearrange("p (kc t) -> p kc t", t=128),
                            in_=h2qt_d.rearrange("(kc p) t -> p kc t", p=128)[:, :, b * S + t0 + sub * 128: b * S + t0 + (sub + 1) * 128]),
                            reads=[r_h2d], writes=[r_h2f[ri]], extra=barA, sem=s_fl[ri], inc=16, tag="h2ld")
                        for si, (n0, n1) in enumerate(NSl):
                            nn = n1 - n0
                            pg = pA[:, (si % 2) * 512:(si % 2) * 512 + nn]
                            pu = pB[:, (si % 2) * 512:(si % 2) * 512 + nn]
                            mmg([(lambda e, kc=kc, n0=n0, n1=n1, pg=pg, ri=ri: e.matmul(
                                pg, h2f[ri][:, kc * 128:(kc + 1) * 128],
                                wqg[:, kc * I + n0: kc * I + n1], start=(kc == 0), stop=(kc == 7)))
                                for kc in range(8)], reads=[r_h2f[ri], r_wqg], writes=[rA[si % 2]], tag="gmm")
                            em("scalar", lambda e, pg=pg, n0=n0, n1=n1: e.activation(
                                hm_tm[:, n0:n1], pg, AF.Silu, bias=0.0, scale=1.0),
                               reads=[rA[si % 2]], writes=[r_hm], tag="silu")
                            mmg([(lambda e, kc=kc, n0=n0, n1=n1, pu=pu, ri=ri: e.matmul(
                                pu, h2f[ri][:, kc * 128:(kc + 1) * 128],
                                wqu[:, kc * I + n0: kc * I + n1], start=(kc == 0), stop=(kc == 7)))
                                for kc in range(8)], reads=[r_h2f[ri], r_wqu], writes=[rB[si % 2]], tag="umm")
                            em("vector", lambda e, pu=pu, n0=n0, n1=n1: e.tensor_tensor(
                                hm_tm[:, n0:n1], hm_tm[:, n0:n1], pu, ALU.mult),
                                reads=[rB[si % 2], r_hm], writes=[r_hm], tag="hmid")
                        quant(hm_tm[:], I, hmq2[sub % 2][:], r_hm, r_hmq2[sub % 2], qsB, tag="hmq")
                        if pend2 is not None:
                            sub_tail(*pend2)
                        if pend is not None:
                            sub_head(*pend)
                        pend2 = pend
                        pend = (b, t0, sub, rx)
            sub_tail(*pend2)
            sub_head(*pend)
            sub_tail(*pend)

            n_out_total = P.scnt[id(s_out)]

        block = eg(nc.Block())
        P.finalize(block, tail={"sync": lambda eng: eng.wait_ge(s_out, n_out_total)})
        global _LAST_PROG, _LAST_SEMS
        _LAST_PROG = P
        _LAST_SEMS = {"s_pe": s_pe, "s_dve": s_dve, "s_act": s_act,
                      "s_c": s_c, "s_iz": s_iz, "s_cat": s_cat}
    return nc


def kernel(L0_attn_x, L0_attn_W_q, L0_attn_W_k, L0_attn_W_v, L0_attn_W_alpha_w,
           L0_attn_W_alpha_b, L0_attn_W_o, L0_ffn_w_gate, L0_ffn_w_up, L0_ffn_w_down):
    n = 8
    Bc = 2
    x = np.ascontiguousarray(L0_attn_x, dtype=np.float32)
    hc = host_consts()
    walb = np.ascontiguousarray(np.asarray(L0_attn_W_alpha_b, np.float32).reshape(4, 128).T)
    common = dict(
        wq_in=np.ascontiguousarray(L0_attn_W_q, np.float32),
        wk_in=np.ascontiguousarray(L0_attn_W_k, np.float32),
        wal_in=np.ascontiguousarray(L0_attn_W_alpha_w, np.float32),
        wv_in=np.ascontiguousarray(L0_attn_W_v, np.float32),
        wo_in=np.ascontiguousarray(L0_attn_W_o, np.float32),
        walb_in=walb,
        wg_in=np.ascontiguousarray(L0_ffn_w_gate, np.float32),
        wu_in=np.ascontiguousarray(L0_ffn_w_up, np.float32),
        wd_in=np.ascontiguousarray(L0_ffn_w_down, np.float32),
        **{k: np.ascontiguousarray(v) for k, v in hc.items()},
    )
    in_maps = []
    for c in range(n):
        m = dict(common)
        m["x_in"] = np.ascontiguousarray(x[c * Bc:(c + 1) * Bc])
        in_maps.append(m)
    nc = build_core(Bc=Bc, S=2048)
    res = run_bass_kernel_spmd(nc, in_maps, core_ids=list(range(n)))
    out = np.concatenate([np.asarray(r["out"], np.float32) for r in res.results], axis=0)
    return out
